# version 14
# baseline (speedup 1.0000x reference)
import numpy as np
import ml_dtypes
import concourse.bass as bass
import concourse.mybir as mybir
from concourse.bass_utils import run_bass_kernel_spmd

F32 = mybir.dt.float32
BF16 = mybir.dt.bfloat16
AF = mybir.ActivationFunctionType
ALU = mybir.AluOpType
AX = mybir.AxisListType
BFNP = ml_dtypes.bfloat16

B, S, D, H, DH, E, FF = 32, 2048, 1024, 16, 64, 8, 2048
NCORE = 8
RSEQ = B // NCORE
NT = S // 128
NC = D // 128
EPS = 1e-5
INVD = 1.0 / D
SCALE = 1.0 / 8.0

ENGS = ("PE", "ACT", "DVE", "POOL", "SP")


class Prog:

    def __init__(self, nc):
        self.nc = nc
        self.ops = []
        self.ranges = {}
        self.partners = {}
        self.wstate = {}
        self.rstate = {}
        self.dma_counts = {}

    def reg(self, h):
        key = id(h)
        if key in self.ranges:
            return h
        try:
            ml = self.nc.lookup_mloc(h)
            mtype = ml.type
        except Exception:
            ml, mtype = None, None
        if mtype == "SB":
            space, lo = "SB", ml.addr
            hi = lo + _ppb(h)
        elif mtype == "PSUM":
            space = "PSUM"
            lo = ml.bank * 2048 + ml.addr
            hi = lo + _ppb(h)
        else:
            space, lo, hi = "DRAM:" + h.name, 0, 1
        self.ranges[key] = (space, lo, hi, h)
        self.partners[key] = set()
        self.wstate[key] = None
        self.rstate[key] = []
        for k2, (sp2, lo2, hi2, _h2) in self.ranges.items():
            if k2 != key and sp2 == space and lo < hi2 and lo2 < hi:
                self.partners[key].add(k2)
                self.partners[k2].add(key)
        return h

    def _keys(self, objs):
        out = []
        for o in objs:
            h = o.tensor if hasattr(o, "tensor") else o
            k = id(h)
            if k not in self.ranges:
                self.reg(h)
            out.append(k)
        return out

    def op(self, eng, fn, reads=(), writes=(), dma_group=None, tag=""):
        opid = len(self.ops)
        rk = self._keys(reads)
        wk = self._keys(writes)
        deps = set()
        for k in rk:
            is_psum = self.ranges[k][0] == "PSUM"
            for g in ({k} | self.partners[k]):
                if self.wstate[g] is not None:
                    deps.add(self.wstate[g])
                if is_psum:
                    by_eng = {}
                    for r in self.rstate[g]:
                        e_r = self.ops[r]["eng"]
                        if e_r != eng and by_eng.get(e_r, -1) < r:
                            by_eng[e_r] = r
                    deps.update(by_eng.values())
        for k in wk:
            for g in ({k} | self.partners[k]):
                if self.wstate[g] is not None:
                    deps.add(self.wstate[g])
                by_eng = {}
                for r in self.rstate[g]:
                    e_r = self.ops[r]["eng"]
                    if by_eng.get(e_r, -1) < r:
                        by_eng[e_r] = r
                deps.update(by_eng.values())
        deps.discard(opid)
        for k in wk:
            for g in ({k} | self.partners[k]):
                self.wstate[g] = opid
                self.rstate[g] = []
        for k in rk:
            self.rstate[k].append(opid)
        gcount = None
        if dma_group is not None:
            self.dma_counts[dma_group] = self.dma_counts.get(dma_group, 0) + 1
            gcount = self.dma_counts[dma_group]
        self.ops.append(dict(eng=eng, fn=fn, deps=deps, dma_group=dma_group,
                             gcount=gcount, tag=tag, milestone=None))
        return opid

    def schedule(self):
        ops = self.ops
        n = len(ops)
        needed = [False] * n
        for o in ops:
            for d in o["deps"]:
                p = ops[d]
                if p["dma_group"] is not None:
                    continue
                if p["eng"] == o["eng"] == "PE":
                    continue
                needed[d] = True
        ms = {e: 0 for e in ENGS}
        for i, o in enumerate(ops):
            if o["dma_group"] is None and needed[i]:
                ms[o["eng"]] += 1
                o["milestone"] = ms[o["eng"]]
        self.ms_total = ms
        know = {e: {} for e in ENGS}
        self_done = {e: 0 for e in ENGS}
        snaps_eng = {}
        snaps_dma = {}
        run_g = {}
        for o in ops:
            e = o["eng"]
            k = know[e]
            wmax = {}
            for d in sorted(o["deps"]):
                p = ops[d]
                if p["dma_group"] is not None:
                    src = ("g", p["dma_group"])
                    val = 16 * run_g.get(p["dma_group"], 0)
                    if k.get(src, 0) >= val:
                        continue
                    wmax[src] = max(wmax.get(src, 0), val)
                    for s2, v2 in snaps_dma.get((p["dma_group"], p["gcount"]),
                                                {}).items():
                        if k.get(s2, 0) < v2:
                            k[s2] = v2
                    if val > 0:
                        k[src] = max(k.get(src, 0), val)
                else:
                    if p["eng"] == e == "PE":
                        continue
                    m = p["milestone"]
                    if m is None:
                        continue
                    src = p["eng"]
                    if src == e:
                        if m <= self_done[e]:
                            continue
                        wmax[src] = max(wmax.get(src, 0), m)
                        self_done[e] = m
                    else:
                        if k.get(src, 0) >= m:
                            continue
                        wmax[src] = max(wmax.get(src, 0), m)
                        k[src] = max(k.get(src, 0), m)
                    for s2, v2 in snaps_eng.get((p["eng"], m), {}).items():
                        if k.get(s2, 0) < v2:
                            k[s2] = v2
            o["waits"] = sorted(wmax.items(), key=lambda x: str(x[0]))
            if o["dma_group"] is not None:
                run_g[o["dma_group"]] = run_g.get(o["dma_group"], 0) + 1
                snaps_dma[(o["dma_group"], o["gcount"])] = dict(k)
            elif o["milestone"] is not None:
                k2 = dict(k)
                k2[e] = o["milestone"]
                snaps_eng[(e, o["milestone"])] = k2
                know[e] = k2
        for e in ENGS:
            assert self.ms_total[e] < 60000, (e, self.ms_total[e])
        for g, c in self.dma_counts.items():
            assert 16 * c < 60000, (g, c)


def _ppb(h):
    free = 1
    for d in list(h.shape)[1:]:
        free *= d
    sz = 2 if h.dtype in (BF16, mybir.dt.float16) else 4
    return free * sz


_PSCNT = [0]


def build_program(verbose=False, nseq=RSEQ,
                  stages=("cast", "wot", "gate", "ln", "attn", "proj",
                          "router", "ffn")):
    nc = bass.Bass("TRN2", target_bir_lowering=False, debug=False,
                   dynamic_dma_scratch_size=2048)
    P = Prog(nc)

    def psname(base):
        _PSCNT[0] += 1
        return f"{base}_{_PSCNT[0]}"

    x_in = nc.dram_tensor("x", [RSEQ, S, D], F32, kind="ExternalInput").ap()
    y_out = nc.dram_tensor("out", [RSEQ, S, D], F32, kind="ExternalOutput").ap()
    wq = nc.dram_tensor("wq", [H, DH, D], F32, kind="ExternalInput").ap()
    wk = nc.dram_tensor("wk", [H, DH, D], F32, kind="ExternalInput").ap()
    wv = nc.dram_tensor("wv", [H, DH, D], F32, kind="ExternalInput").ap()
    wo = nc.dram_tensor("wo", [H, D, DH], F32, kind="ExternalInput").ap()
    gamma = nc.dram_tensor("gamma", [1, D], F32, kind="ExternalInput").ap()
    beta = nc.dram_tensor("beta", [1, D], F32, kind="ExternalInput").ap()
    gw = nc.dram_tensor("gw", [E, D], F32, kind="ExternalInput").ap()
    w1 = nc.dram_tensor("w1", [E, FF, D], F32, kind="ExternalInput").ap()
    w2 = nc.dram_tensor("w2", [E, D, FF], F32, kind="ExternalInput").ap()

    wq_bf = nc.dram_tensor("wq_bf", [H * DH, D], BF16, kind="Internal").ap()
    wk_bf = nc.dram_tensor("wk_bf", [H * DH, D], BF16, kind="Internal").ap()
    wv_bf = nc.dram_tensor("wv_bf", [H * DH, D], BF16, kind="Internal").ap()
    wot_bf = nc.dram_tensor("wot_bf", [H * DH, D], BF16, kind="Internal").ap()
    wo_bf = nc.dram_tensor("wo_bf", [H, D, DH], BF16, kind="Internal").ap()
    w1_bf = [nc.dram_tensor(f"w1_bf{e}", [FF, D], BF16, kind="Internal").ap()
             for e in range(E)]
    w2_bf = [nc.dram_tensor(f"w2_bf{e}", [D, FF], BF16, kind="Internal").ap()
             for e in range(E)]

    ident_bf_h = nc.inline_tensor(np.eye(128, dtype=BFNP), name="ident_bf")
    ident_f32_h = nc.inline_tensor(np.eye(128, dtype=np.float32), name="ident_f32")
    maskU_h = nc.inline_tensor(np.triu(np.ones((128, 128), dtype=BFNP)),
                               name="maskU")
    ones_row_bf_h = nc.inline_tensor(np.ones((1, 128), dtype=BFNP),
                                     name="ones_row_bf")
    ones_row_f32_h = nc.inline_tensor(np.ones((1, 128), dtype=np.float32),
                                      name="ones_row_f32")
    ones_col_bf_h = nc.inline_tensor(np.ones((128, 1), dtype=BFNP),
                                     name="ones_col_bf")
    basis_np = np.zeros((E, E * 128), dtype=BFNP)
    for e in range(E):
        basis_np[e, e * 128:(e + 1) * 128] = 1
    basis_h = nc.inline_tensor(basis_np, name="basis")
    constv_h = nc.inline_tensor(
        np.concatenate([np.zeros((128, 1), np.float32),
                        np.full((128, 1), EPS, np.float32)], axis=1),
        name="constv")

    A = nc.alloc_sbuf_tensor
    x_fm = [A(f"x_fm{c}", [128, S], F32) for c in range(NC)]
    xn_fm = [A(f"xn_fm{c}", [128, S], BF16) for c in range(NC)]
    zh = [A(f"zh{c}", [128, S], BF16) for c in range(NC)]
    vtok = [A(f"vtok{k}", [128, H * (DH + 1)], BF16) for k in range(NT)]
    q_c = A("q_c", [128, S], BF16)
    k_c = A("k_c", [128, S], BF16)
    attnT = A("attnT", [128, 2048], BF16)
    w1tA = [A(f"w1tA{c}", [128, 1024], BF16) for c in range(NC)]
    w2tA = [A(f"w2tA{c}", [128, 1024], BF16) for c in range(NC)]
    comb_rep = A("comb_rep", [128, S], BF16)
    comb_tok = A("comb_tok", [128, NT * E], F32)
    rstd3_tok = A("rstd3_tok", [128, NT], F32)
    ct_tok = A("ct_tok", [128, NT], F32)
    out_stage = [A(f"out_stage{i}", [128, D], F32) for i in range(2)]
    rt_scr = A("rt_scr", [128, 32], F32)
    cbuf = A("cbuf", [128, 2], F32)
    ident_bf = A("identbf_sb", [128, 128], BF16)
    ident_f32 = A("identf32_sb", [128, 128], F32)
    maskU = A("maskU_sb", [128, 128], BF16)
    ones_row_bf = A("onesrow_bf_sb", [1, 128], BF16)
    ones_row_f32 = A("onesrow_f32_sb", [1, 128], F32)
    ones_col_bf = A("onescol_bf_sb", [128, 1], BF16)
    basis_sb = A("basis_sb", [E, E * 128], BF16)
    gwT = [A(f"gwT{c}", [128, E], BF16) for c in range(NC)]
    sp_rep = A("sp_rep", [128, E], F32)
    bp_rep = A("bp_rep", [128, E], F32)

    def addr(h):
        return nc.lookup_mloc(h).addr

    def at(name, shape, dtype, base_h, delta=0):
        return nc.alloc_sbuf_tensor_at(name, shape, dtype,
                                       offset=addr(base_h) + delta)

    for i in range(1, NT):
        assert addr(vtok[i]) == addr(vtok[0]) + i * 2080

    gw_f32 = at("gw_f32", [E, D], F32, zh[2], 0)
    gamma_row = at("gamma_row", [1, D], F32, zh[3], 0)
    beta_row = at("beta_row", [1, D], F32, zh[4], 0)
    ph0_stage = at("ph0_stage", [128, 256], BF16, zh[5], 0)
    small_rows = at("small_rows", [8, 128], F32, zh[5], 512)
    r_rows = [at(f"r_row{i}", [1, 512], F32, zh[6], i * 2048) for i in range(2)]
    tmp_strip = at("tmp_strip", [128, 512], F32, zh[7], 0)
    sq_strip = [at("sq_strip0", [128, 512], BF16, zh[7], 2048),
                at("sq_strip1", [128, 512], BF16, zh[7], 3072)]
    cast_strip = [at("cast_strip0", [128, 512], BF16, zh[6], 0),
                  at("cast_strip1", [128, 512], BF16, zh[6], 1024)]

    x_stage = [at("x_stage0", [128, D], F32, q_c, 0),
               at("x_stage1", [128, D], F32, k_c, 0)]
    rowA_bf = at("rowA_bf", [1, 1024], BF16, comb_rep, 0)
    rowB_bf = at("rowB_bf", [1, 1024], BF16, comb_rep, 2048)
    repA_sb = at("repA_sb", [64, 1024], BF16, comb_rep, 0)
    repB_sb = at("repB_sb", [64, 1024], BF16, comb_rep, 2048)
    comb_T = at("comb_T", [E, S], BF16, attnT, 0)
    wsl = [[at(f"wsl{p}_{j}", [128, 1024], BF16, w2tA[p * 3 + j], 0)
            for j in range(3)] for p in range(2)]
    wot_sb = [at(f"wot_sb{c}", [128, 1024], BF16, w1tA[c], 0) for c in range(NC)]
    w1tB = [at(f"w1tB{c}", [128, 1024], BF16, vtok[0], c * 2048)
            for c in range(NC)]
    w2tB = [at(f"w2tB{c}", [128, 1024], BF16, vtok[8], c * 2048)
            for c in range(NC)]

    if verbose:
        print("SBUF bytes remaining:", nc.sbuf_bytes_remaining)
    assert nc.sbuf_bytes_remaining >= 0
    nc.const_aps.register(F32, 0.0, cbuf.ap()[:, 0:1])
    nc.const_aps.register(F32, EPS, cbuf.ap()[:, 1:2])

    def dma(group, out, in_, transpose=False, queue="SP", tag=""):
        def fn(eng, out=out, in_=in_, transpose=transpose):
            return eng.dma_start(out=out, in_=in_, transpose=transpose)
        P.op(queue, fn, reads=[in_], writes=[out], dma_group=group, tag=tag)

    def mm(out, lhsT, rhs, start, stop, tp=None, tag=""):
        def fn(eng, out=out, lhsT=lhsT, rhs=rhs, start=start, stop=stop, tp=tp):
            return eng.matmul(out=out, lhsT=lhsT, rhs=rhs, start=start,
                              stop=stop, tile_position=tp)
        P.op("PE", fn, reads=[lhsT, rhs], writes=[out], tag=tag)

    def tr(out, in_, ident, tag=""):
        def fn(eng, out=out, in_=in_, ident=ident):
            return eng.transpose(out=out, in_=in_, identity=ident)
        P.op("PE", fn, reads=[in_, ident], writes=[out], tag=tag)

    def act(out, in_, func, bias=0.0, scale=1.0, tag=""):
        reads = [in_] + ([bias] if hasattr(bias, "tensor") else [cbuf])
        def fn(e, out=out, in_=in_, func=func, bias=bias, scale=scale):
            return e.activation(out=out, in_=in_, func=func, bias=bias,
                                scale=scale)
        P.op("ACT", fn, reads=reads, writes=[out], tag=tag)

    def copy(out, in_, eng="DVE", tag=""):
        if eng == "ACT":
            def fn(e, out=out, in_=in_):
                return e.copy(out=out, in_=in_)
        else:
            def fn(e, out=out, in_=in_):
                return e.tensor_copy(out=out, in_=in_)
        P.op(eng, fn, reads=[in_], writes=[out], tag=tag)

    def tt(out, in0, in1, opk, eng="DVE", tag=""):
        def fn(e, out=out, in0=in0, in1=in1, opk=opk):
            return e.tensor_tensor(out=out, in0=in0, in1=in1, op=opk)
        P.op(eng, fn, reads=[in0, in1], writes=[out], tag=tag)

    def ts(out, in0, s1, s2, op0, op1=ALU.bypass, eng="DVE", tag=""):
        reads = [in0] + [s for s in (s1, s2) if hasattr(s, "tensor")]
        def fn(e, out=out, in0=in0, s1=s1, s2=s2, op0=op0, op1=op1):
            return e.tensor_scalar(out=out, in0=in0, scalar1=s1, scalar2=s2,
                                   op0=op0, op1=op1)
        P.op(eng, fn, reads=reads, writes=[out], tag=tag)

    def red(out, in_, opk, negate=False, tag=""):
        def fn(e, out=out, in_=in_, opk=opk, negate=negate):
            return e.tensor_reduce(out=out, in_=in_, axis=AX.X, op=opk,
                                   negate=negate)
        P.op("DVE", fn, reads=[in_], writes=[out], tag=tag)

    def recip(out, in_, tag=""):
        if out.dtype == mybir.dt.float32r:
            def fn(e, out=out, in_=in_):
                with nc.allow_low_precision(reason="f32r recip for replication"):
                    return e.reciprocal(out=out, in_=in_)
        else:
            def fn(e, out=out, in_=in_):
                return e.reciprocal(out=out, in_=in_)
        P.op("DVE", fn, reads=[in_], writes=[out], tag=tag)

    def memset(buf, val, tag=""):
        def fn(e, buf=buf, val=val):
            return e.memset(buf, val)
        P.op("DVE", fn, reads=[], writes=[buf], tag=tag)

    dma("g_const", ident_bf[:], ident_bf_h.ap())
    dma("g_const", ident_f32[:], ident_f32_h.ap())
    dma("g_const", maskU[:], maskU_h.ap())
    dma("g_const", ones_row_bf[:], ones_row_bf_h.ap())
    dma("g_const", ones_row_f32[:], ones_row_f32_h.ap())
    dma("g_const", ones_col_bf[:], ones_col_bf_h.ap())
    dma("g_const", basis_sb[:], basis_h.ap())
    dma("g_const", gw_f32[:], gw)
    dma("g_const", cbuf[:], constv_h.ap())
    dma("g_const", gamma_row[:], gamma)
    dma("g_const", beta_row[:], beta)

    if "cast" in stages:
        dma("g_ph0", wq_bf, wq.rearrange("h d c -> (h d) c"), queue="POOL")
        dma("g_ph0", wk_bf, wk.rearrange("h d c -> (h d) c"), queue="POOL")
        dma("g_ph0", wv_bf, wv.rearrange("h d c -> (h d) c"), queue="POOL")
        dma("g_ph0", wo_bf, wo, queue="POOL")
        for e in range(E):
            dma("g_ph0", w1_bf[e], w1[e], queue="POOL")
            dma("g_ph0", w2_bf[e], w2[e], queue="POOL")

    if "wot" in stages:
      with nc.psum_tensor(psname("ps_wot"), [64, 512], BF16) as ps_wot:
          P.reg(ps_wot)
          for a in range(H):
              for et in range(NC):
                  dma("g_ph0b", ph0_stage[:, 0:64],
                      wo_bf[a, et * 128:(et + 1) * 128, :])
                  sl = ps_wot[0:64, (et % 4) * 128:(et % 4) * 128 + 128]
                  tr(sl, ph0_stage[:, 0:64], ident_bf[:], tag=f"wotT{a}_{et}")
                  copy(ph0_stage[0:64, 128:256], sl, eng="ACT")
                  dma("g_ph0b",
                      wot_bf[a * 64:(a + 1) * 64, et * 128:(et + 1) * 128],
                      ph0_stage[0:64, 128:256])

    if "gate" in stages:
      with nc.psum_tensor(psname("ps_g"), [128, 1024], F32) as ps_g:
          P.reg(ps_g)
          for nch in range(2):
              mm(ps_g[0:8, nch * 512:(nch + 1) * 512], ones_row_f32[0:1, 0:8],
                 gamma_row[0:1, nch * 512:(nch + 1) * 512], True, True, tag="grep")
          tt(gw_f32[:], gw_f32[:], ps_g[0:8, 0:1024], ALU.mult, tag="gwp")
      gw_bf = at("gw_bf", [E, D], BF16, zh[0], 0)
      P.reg(gw_bf)
      copy(gw_bf[:], gw_f32[:], eng="ACT", tag="gwbf")
      with nc.psum_tensor(psname("ps_gt"), [128, 512], BF16) as ps_gt:
          P.reg(ps_gt)
          for c in range(NC):
              tr(ps_gt[:, c * 8:(c + 1) * 8], gw_bf[0:8, c * 128:(c + 1) * 128],
                 ident_bf[0:8, 0:8], tag=f"gwT{c}")
          for c in range(NC):
              copy(gwT[c][:], ps_gt[:, c * 8:(c + 1) * 8], eng="DVE")
      beta_bf = at("beta_bf", [1, D], BF16, zh[1], 0)
      beta_fm = at("beta_fm", [128, 8], BF16, zh[1], 2048)
      P.reg(beta_bf); P.reg(beta_fm)
      copy(beta_bf[:], beta_row[:], eng="ACT", tag="betabf")
      with nc.psum_tensor(psname("ps_bt"), [128, 512], BF16) as ps_bt:
          P.reg(ps_bt)
          for c in range(NC):
              tr(ps_bt[:, 2 * c:2 * c + 1], beta_bf[0:1, c * 128:(c + 1) * 128],
                 ident_bf[0:1, 0:1], tag=f"betaT{c}")
          copy(beta_fm[:],
               ps_bt.ap()[:, 0:16].rearrange("p (a b) -> p a b", b=2)[:, :, 0],
               eng="DVE")
      with nc.psum_tensor(psname("ps_bs"), [8, 512], F32) as ps_bs:
          P.reg(ps_bs)
          for c in range(NC):
              mm(ps_bs[0:8, 0:1], gwT[c][:], beta_fm[:, c:c + 1],
                 c == 0, c == NC - 1, tag="bp")
          for c in range(NC):
              mm(ps_bs[0:8, 1:2], gwT[c][:], ones_col_bf[:],
                 c == 0, c == NC - 1, tag="sp")
          copy(small_rows[0:8, 0:2], ps_bs[0:8, 0:2], eng="DVE")
      with nc.psum_tensor(psname("ps_br"), [128, 512], F32) as ps_br:
          P.reg(ps_br)
          tr(ps_br[0:1, 0:8], small_rows[0:8, 0:1], ident_f32[0:8, 0:8], tag="bT")
          tr(ps_br[0:1, 8:16], small_rows[0:8, 1:2], ident_f32[0:8, 0:8], tag="sT")
          copy(small_rows[0:1, 8:24], ps_br[0:1, 0:16], eng="DVE")
          mm(ps_br[:, 16:24], ones_row_f32[0:1, :], small_rows[0:1, 8:16],
             True, True, tag="bprep")
          mm(ps_br[:, 24:32], ones_row_f32[0:1, :], small_rows[0:1, 16:24],
             True, True, tag="sprep")
          copy(bp_rep[:], ps_br[:, 16:24], eng="DVE")
          copy(sp_rep[:], ps_br[:, 24:32], eng="DVE")


    def ln_to(src, dst, src_f32, stats_out=False):
        with nc.psum_tensor(psname("ps_sum"), [1, 512], F32) as ps_sum, \
             nc.psum_tensor(psname("ps_sq"), [1, 512], F32) as ps_sq, \
             nc.psum_tensor(psname("ps_mu"), [128, 512], F32) as ps_mu, \
             nc.psum_tensor(psname("ps_rs"), [128, 512], F32) as ps_rs:
            P.reg(ps_sum); P.reg(ps_sq); P.reg(ps_mu); P.reg(ps_rs)
            for st in range(4):
                sl = slice(st * 512, (st + 1) * 512)
                for c in range(NC):
                    if src_f32:
                        xbf = cast_strip[c % 2][:]
                        act(xbf, src[c][:, sl], AF.Copy, tag=f"lncast{st}_{c}")
                    else:
                        xbf = src[c][:, sl]
                    mm(ps_sum[0:1, :], ones_col_bf[:], xbf, c == 0, c == NC - 1,
                       tag=f"lnsum{st}_{c}")
                    act(sq_strip[c % 2][:], xbf, AF.Square, tag=f"lnsq{st}_{c}")
                    mm(ps_sq[0:1, :], ones_col_bf[:], sq_strip[c % 2][:],
                       c == 0, c == NC - 1, tag=f"lnsq2{st}_{c}")
                ts(r_rows[0][:], ps_sum[0:1, :], INVD, None, ALU.mult,
                   tag=f"lnmu{st}")
                ts(r_rows[1][:], ps_sq[0:1, :], INVD, None, ALU.mult,
                   tag=f"lnmsq{st}")
                tt(tmp_strip[0:1, :], r_rows[0][:], r_rows[0][:], ALU.mult,
                   tag=f"lnmu2{st}")
                tt(r_rows[1][:], r_rows[1][:], tmp_strip[0:1, :], ALU.subtract,
                   tag=f"lnvar{st}")
                act(tmp_strip[0:1, :], r_rows[1][:], AF.Ln, bias=EPS,
                    tag=f"lnln{st}")
                act(r_rows[1][:], tmp_strip[0:1, :], AF.Exp, scale=-0.5,
                    tag=f"lnexp{st}")
                if stats_out:
                    tt(tmp_strip[0:1, :], r_rows[0][:], r_rows[1][:], ALU.mult,
                       tag=f"ct{st}")
                    with nc.psum_tensor(psname("ps_stt"), [128, 8], F32) as pst:
                        P.reg(pst)
                        for q in range(4):
                            tr(pst[:, q * 2:q * 2 + 1],
                               r_rows[1][0:1, q * 128:(q + 1) * 128],
                               ident_f32[0:1, 0:1], tag=f"rsT{st}_{q}")
                            tr(pst[:, q * 2 + 1:q * 2 + 2],
                               tmp_strip[0:1, q * 128:(q + 1) * 128],
                               ident_f32[0:1, 0:1], tag=f"ctT{st}_{q}")
                        for q in range(4):
                            tile = st * 4 + q
                            copy(rstd3_tok[:, tile:tile + 1],
                                 pst[:, q * 2:q * 2 + 1])
                            copy(ct_tok[:, tile:tile + 1],
                                 pst[:, q * 2 + 1:q * 2 + 2])
                if dst is not None:
                    mm(ps_mu[:, :], ones_row_f32[0:1, :], r_rows[0][:],
                       True, True, tag=f"murep{st}")
                    mm(ps_rs[:, :], ones_row_f32[0:1, :], r_rows[1][:],
                       True, True, tag=f"rsrep{st}")
                    for c in range(NC):
                        tt(tmp_strip[:], src[c][:, sl], ps_mu[:, :],
                           ALU.subtract, tag=f"lnsub{st}_{c}")
                        tt(dst[c][:, sl], tmp_strip[:], ps_rs[:, :], ALU.mult,
                           tag=f"lnmul{st}_{c}")

    def load_and_ln1(s):
        with nc.psum_tensor(psname("ps_ld0"), [128, 1024], F32) as p0, \
             nc.psum_tensor(psname("ps_ld1"), [128, 1024], F32) as p1:
            P.reg(p0); P.reg(p1)
            pss = [p0, p1]
            for t in range(NT):
                stg = x_stage[t % 2]
                dma(f"g_xs{t % 2}", stg[:], x_in[s, t * 128:(t + 1) * 128, :],
                    tag=f"xload{t}")
                ps = pss[t % 2]
                for c in range(NC):
                    tr(ps[:, c * 128:(c + 1) * 128], stg[:, c * 128:(c + 1) * 128],
                       ident_f32[:], tag=f"xtr{t}_{c}")
                for c in range(NC):
                    copy(x_fm[c][:, t * 128:(t + 1) * 128],
                         ps[:, c * 128:(c + 1) * 128],
                         eng=("DVE" if c % 2 == 0 else "ACT"), tag=f"xcp{t}_{c}")
        if "ln" in stages:
            ln_to(x_fm, xn_fm, src_f32=True)

    def qkv_attn(s):
        for c in range(NC):
            dma("g_wot", wot_sb[c][:], wot_bf[c * 128:(c + 1) * 128, :],
                tag=f"wot{c}")
        for kt in range(NT):
            memset(vtok[kt].ap().rearrange("p (h v) -> p h v", v=DH + 1)
                   [:, :, DH], 1.0, tag=f"vones{kt}")
        def wsl_load(g):
            par = g % 2
            for c in range(NC):
                dma(f"g_wsl{par}", wsl[par][0][:, c * 128:(c + 1) * 128],
                    wq_bf[g * 128:(g + 1) * 128, c * 128:(c + 1) * 128],
                    transpose=True, tag=f"wq{g}_{c}")
                dma(f"g_wsl{par}", wsl[par][1][:, c * 128:(c + 1) * 128],
                    wk_bf[g * 128:(g + 1) * 128, c * 128:(c + 1) * 128],
                    transpose=True, tag=f"wk{g}_{c}")
                dma(f"g_wsl{par}", wsl[par][2][:, c * 128:(c + 1) * 128],
                    wv_bf[g * 128:(g + 1) * 128, c * 128:(c + 1) * 128],
                    transpose=True, tag=f"wv{g}_{c}")

        wsl_load(0)
        for g in range(NC):
            par = g % 2
            with nc.psum_tensor(psname("ps_q"), [128, S], F32) as ps_q, \
                 nc.psum_tensor(psname("ps_k"), [128, S], F32) as ps_k:
                P.reg(ps_q); P.reg(ps_k)
                for c in range(NC):
                    for nch in range(4):
                        mm(ps_q[:, nch * 512:(nch + 1) * 512],
                           wsl[par][0][:, c * 128:(c + 1) * 128],
                           xn_fm[c][:, nch * 512:(nch + 1) * 512],
                           c == 0, c == NC - 1, tag=f"qmm{g}")
                for i in range(2):
                    copy(q_c[:, i * 1024:(i + 1) * 1024],
                         ps_q[:, i * 1024:(i + 1) * 1024], eng="ACT",
                         tag=f"qev{g}")
                for c in range(NC):
                    for nch in range(4):
                        mm(ps_k[:, nch * 512:(nch + 1) * 512],
                           wsl[par][1][:, c * 128:(c + 1) * 128],
                           xn_fm[c][:, nch * 512:(nch + 1) * 512],
                           c == 0, c == NC - 1, tag=f"kmm{g}")
                for i in range(2):
                    copy(k_c[:, i * 1024:(i + 1) * 1024],
                         ps_k[:, i * 1024:(i + 1) * 1024], eng="ACT",
                         tag=f"kev{g}")
            with nc.psum_tensor(psname("ps_v0"), [128, 128], F32) as pv0, \
                 nc.psum_tensor(psname("ps_v1"), [128, 128], F32) as pv1, \
                 nc.psum_tensor(psname("ps_v2"), [128, 128], F32) as pv2, \
                 nc.psum_tensor(psname("ps_v3"), [128, 128], F32) as pv3:
                pvs = [pv0, pv1, pv2, pv3]
                for p_ in pvs:
                    P.reg(p_)
                for kt in range(NT):
                    sl_ps = pvs[kt % 4][:, :]
                    for c in range(NC):
                        mm(sl_ps, xn_fm[c][:, kt * 128:(kt + 1) * 128],
                           wsl[par][2][:, c * 128:(c + 1) * 128],
                           c == 0, c == NC - 1,
                           tag=f"vmm{g}_{kt}")
                    vv = vtok[kt].ap().rearrange("p (h v) -> p h v", v=DH + 1)
                    copy(vv[:, 2 * g:2 * g + 2, 0:DH],
                         sl_ps.rearrange("p (h v) -> p h v", v=DH)[:, 0:2, :],
                         eng="DVE", tag=f"vev{g}_{kt}")
            if g + 1 < NC:
                wsl_load(g + 1)
            attn_pair(s, g)

    def attn_pair(s, g):
        with nc.psum_tensor(psname("ps_s"), [128, 2048], F32) as ps_s, \
             nc.psum_tensor(psname("ps_za"), [65, 1024], F32) as ps_za, \
             nc.psum_tensor(psname("ps_zb"), [65, 1024], F32) as ps_zb:
            P.reg(ps_s); P.reg(ps_za); P.reg(ps_zb)
            psz = [ps_za, ps_zb]
            for half in range(2):
                h_lo = half * 1024
                nkb = 8 * (half + 1)
                for kb in range(nkb):
                    t1_lo = max(kb * 128 - h_lo, 0)
                    for hh in range(2):
                        base = hh * 64
                        col = t1_lo
                        while col < 1024:
                            w = min(512 - col % 512, 1024 - col)
                            mm(ps_s[:, hh * 1024 + col:hh * 1024 + col + w],
                               q_c[base:base + 64, kb * 128:(kb + 1) * 128],
                               k_c[base:base + 64, h_lo + col:h_lo + col + w],
                               True, True, tp=(base, 0),
                               tag=f"sc{g}_{half}_{kb}_{hh}")
                            col += w
                    wv_ = 1024 - t1_lo
                    exp_in = ps_s.ap().rearrange("p (h t) -> p h t", h=2)[
                        :, :, t1_lo:1024]
                    exp_out = attnT.ap().rearrange("p (h t) -> p h t", h=2)[
                        :, :, t1_lo:1024]
                    act(exp_out, exp_in, AF.Exp, scale=SCALE,
                        tag=f"exp{g}_{half}_{kb}")
                    if kb * 128 >= h_lo:
                        dcol = kb * 128 - h_lo
                        for hh in range(2):
                            tt(attnT[:, hh * 1024 + dcol:hh * 1024 + dcol + 128],
                               attnT[:, hh * 1024 + dcol:hh * 1024 + dcol + 128],
                               maskU[:], ALU.mult, tag=f"msk{g}_{half}_{kb}_{hh}")
                    for hh in range(2):
                        head = 2 * g + hh
                        ps_z = psz[hh]
                        vvh = vtok[kb][:, head * (DH + 1):(head + 1) * (DH + 1)]
                        col = t1_lo
                        while col < 1024:
                            w = min(512 - col % 512, 1024 - col)
                            mm(ps_z[:, col:col + w], vvh,
                               attnT[:, hh * 1024 + col:hh * 1024 + col + w],
                               kb == 0, kb == nkb - 1,
                               tag=f"av{g}_{half}_{kb}_{hh}")
                            col += w
                recip(out_stage[0][0:1, 0:1024], ps_za[64:65, :],
                      tag=f"recipA{g}_{half}")
                recip(out_stage[1][0:1, 0:1024], ps_zb[64:65, :],
                      tag=f"recipB{g}_{half}")
                copy(rowA_bf[:], out_stage[0][0:1, 0:1024], eng="ACT",
                     tag=f"rcA{g}_{half}")
                copy(rowB_bf[:], out_stage[1][0:1, 0:1024], eng="ACT",
                     tag=f"rcB{g}_{half}")
                for i in range(2):
                    mm(ps_s[0:64, i * 512:(i + 1) * 512],
                       ones_row_bf[0:1, 0:64], rowA_bf[0:1, i * 512:(i + 1) * 512],
                       True, True, tag=f"rrepA{g}_{half}_{i}")
                for i in range(2):
                    mm(ps_s[0:64, 1024 + i * 512:1024 + (i + 1) * 512],
                       ones_row_bf[0:1, 0:64], rowB_bf[0:1, i * 512:(i + 1) * 512],
                       True, True, tag=f"rrepB{g}_{half}_{i}")
                copy(repA_sb[:], ps_s[0:64, 0:1024], eng="ACT",
                     tag=f"rrcpA{g}_{half}")
                copy(repB_sb[:], ps_s[0:64, 1024:2048], eng="ACT",
                     tag=f"rrcpB{g}_{half}")
                tt(zh[g][0:64, half * 1024:(half + 1) * 1024],
                   ps_za[0:64, :], repA_sb[:], ALU.mult,
                   tag=f"znormA{g}_{half}")
                tt(zh[g][64:128, half * 1024:(half + 1) * 1024],
                   ps_zb[0:64, :], repB_sb[:], ALU.mult,
                   tag=f"znormB{g}_{half}")

    def outproj(s):
        with nc.psum_tensor(psname("ps_o0"), [128, S], F32) as p0, \
             nc.psum_tensor(psname("ps_o1"), [128, S], F32) as p1:
            P.reg(p0); P.reg(p1)
            pss = [p0, p1]
            for ec in range(NC):
                ps = pss[ec % 2]
                for ac in range(NC):
                    for nch in range(4):
                        mm(ps[:, nch * 512:(nch + 1) * 512],
                           wot_sb[ac][:, ec * 128:(ec + 1) * 128],
                           zh[ac][:, nch * 512:(nch + 1) * 512],
                           ac == 0, ac == NC - 1, tag=f"omm{ec}")
                for i in range(2):
                    tt(x_fm[ec][:, i * 1024:(i + 1) * 1024],
                       x_fm[ec][:, i * 1024:(i + 1) * 1024],
                       ps[:, i * 1024:(i + 1) * 1024], ALU.add,
                       tag=f"ores{ec}_{i}")

    def router(s):
        ln_to(x_fm, xn_fm, src_f32=True)
        ln_to(xn_fm, None, src_f32=False, stats_out=True)
        with nc.psum_tensor(psname("ps_lg"), [128, 128], F32) as ps_lg, \
             nc.psum_tensor(psname("ps_ct"), [8, 512], BF16) as ps_ct:
            P.reg(ps_lg); P.reg(ps_ct)
            lg_sb = at(psname("lg_sb"), [128, NT * E], F32, comb_rep, 0)
            P.reg(lg_sb)
            for t in range(NT):
                for c in range(NC):
                    mm(ps_lg[:, t * 8:(t + 1) * 8],
                       xn_fm[c][:, t * 128:(t + 1) * 128],
                       gwT[c][:], c == 0, c == NC - 1, tag=f"lg{t}")
                ts(lg_sb[:, t * 8:(t + 1) * 8], ps_lg[:, t * 8:(t + 1) * 8],
                   rstd3_tok[:, t:t + 1], None, ALU.mult, tag=f"lga{t}")
            for t in range(NT):
                cs = comb_tok[:, t * 8:(t + 1) * 8]
                lgs = lg_sb[:, t * 8:(t + 1) * 8]
                ts(rt_scr[:, 0:8], sp_rep[:], ct_tok[:, t:t + 1], None,
                   ALU.mult, tag=f"lgb{t}")
                tt(cs, lgs, rt_scr[:, 0:8], ALU.subtract, tag=f"lgc{t}")
                tt(cs, cs, bp_rep[:], ALU.add, tag=f"lgd{t}")
                nm = rt_scr[:, 8:9]
                red(nm, cs, ALU.max, negate=True, tag=f"smm{t}")
                act(cs, cs, AF.Exp, bias=nm, tag=f"sme{t}")
                sm = rt_scr[:, 9:10]
                red(sm, cs, ALU.add, tag=f"sms{t}")
                rc = rt_scr[:, 10:11]
                recip(rc, sm, tag=f"smr{t}")
                ts(cs, cs, rc, None, ALU.mult, tag=f"smn{t}")
                m1 = rt_scr[:, 11:12]
                red(m1, cs, ALU.max, tag=f"t2a{t}")
                eq = rt_scr[:, 16:24]
                ts(eq, cs, m1, None, ALU.is_ge, tag=f"t2b{t}")
                ts(eq, eq, -1.0, 1.0, ALU.mult, ALU.add, tag=f"t2c{t}")
                tt(eq, eq, cs, ALU.mult, tag=f"t2d{t}")
                m2 = rt_scr[:, 12:13]
                red(m2, eq, ALU.max, tag=f"t2e{t}")
                ts(eq, cs, m2, None, ALU.is_ge, tag=f"t2f{t}")
                tt(cs, cs, eq, ALU.mult, tag=f"t2g{t}")
                cbf = rt_scr.ap().bitcast(BF16)[:, 48:56]
                copy(cbf, cs, eng="DVE", tag=f"cbf{t}")
                tr(ps_ct[0:8, (t % 4) * 128:(t % 4) * 128 + 128], cbf,
                   ident_bf[:], tag=f"cT{t}")
                copy(comb_T[0:8, t * 128:(t + 1) * 128],
                     ps_ct[0:8, (t % 4) * 128:(t % 4) * 128 + 128], eng="DVE",
                     tag=f"cTc{t}")

    def ffn(s):
        def gbufs(gi):
            return (w1tB, w2tB) if gi % 2 == 0 else (w1tA, w2tA)

        def load_granule(gi):
            e, half = gi // 2, gi % 2
            w1b, w2b = gbufs(gi)
            grp = f"g_gr{gi % 2}"
            for c in range(NC):
                dma(grp, w1b[c][:],
                    w1_bf[e][half * 1024:(half + 1) * 1024,
                             c * 128:(c + 1) * 128], transpose=True,
                    tag=f"w1g{gi}_{c}")
            for fc in range(NC):
                dma(grp, w2b[fc][:],
                    w2_bf[e][:, half * 1024 + fc * 128:
                             half * 1024 + (fc + 1) * 128], transpose=True,
                    tag=f"w2g{gi}_{fc}")

        load_granule(0)
        load_granule(1)
        if "dbgw" in stages and s == 0:
            dma("g_os0", dbgw[0], w1tB[0][:], tag="dbgw0")
            dma("g_os0", dbgw[1], w1tB[5][:], tag="dbgw1")
            dma("g_os0", dbgw[2], w2tB[0][:], tag="dbgw2")
            dma("g_os0", dbgw[3], w1tA[0][:], tag="dbgw3")
        with nc.psum_tensor(psname("ps_h0"), [128, 1024], F32) as ph0, \
             nc.psum_tensor(psname("ps_h1"), [128, 1024], F32) as ph1, \
             nc.psum_tensor(psname("ps_y0"), [128, 1024], F32) as py0, \
             nc.psum_tensor(psname("ps_y1"), [128, 1024], F32) as py1:
            P.reg(ph0); P.reg(ph1); P.reg(py0); P.reg(py1)
            psh = [ph0, ph1]
            psy = [py0, py1]
            for e in range(E):
                for i in range(4):
                    mm(ph0[:, (i % 2) * 512:(i % 2) * 512 + 512],
                       basis_sb[0:8, e * 128:(e + 1) * 128],
                       comb_T[0:8, i * 512:(i + 1) * 512], True, True,
                       tag=f"crep{e}_{i}")
                    copy(comb_rep[:, i * 512:(i + 1) * 512],
                         ph0[:, (i % 2) * 512:(i % 2) * 512 + 512], eng="DVE",
                         tag=f"crepc{e}_{i}")
                for half in range(2):
                    gi = e * 2 + half
                    w1b, w2b = gbufs(gi)
                    if "dbgh" in stages and s == 0 and e == 0 and half == 0:
                        pass
                    for fc in range(NC):
                        for tb in range(2):
                            ps = psh[(fc * 2 + tb) % 2]
                            for th in range(2):
                                for c in range(NC):
                                    mm(ps[:, th * 512:(th + 1) * 512],
                                       w1b[c][:, fc * 128:(fc + 1) * 128],
                                       xn_fm[c][:, tb * 1024 + th * 512:
                                                tb * 1024 + (th + 1) * 512],
                                       c == 0, c == NC - 1,
                                       tag=f"h{gi}_{fc}_{tb}_{th}")
                            act(zh[fc][:, tb * 1024:(tb + 1) * 1024], ps[:, :],
                                AF.Gelu, tag=f"gelu{gi}_{fc}_{tb}")
                        tt(zh[fc][:], zh[fc][:], comb_rep[:], ALU.mult,
                           tag=f"hscale{gi}_{fc}")
                    if "dbgh" in stages and s == 0 and e == 0 and half == 0:
                        for c2 in range(NC):
                            dma("g_os0", dbgh[c2], zh[c2][:], tag=f"dbgh{c2}")
                        dma("g_os0", dbghc, comb_rep[:], tag="dbghc")
                        dma("g_os0", dbghw[0], w1tB[0][:], tag="dbghw0")
                        dma("g_os0", dbghw[1], xn_fm[0][:, 0:1024], tag="dbghw1")
                    for cc in range(NC):
                        for i in range(2):
                            ps_y = psy[(cc * 2 + i) % 2]
                            for fc in range(NC):
                                for nch in range(2):
                                    mm(ps_y[:, nch * 512:(nch + 1) * 512],
                                       w2b[fc][:, cc * 128:(cc + 1) * 128],
                                       zh[fc][:, i * 1024 + nch * 512:
                                               i * 1024 + (nch + 1) * 512],
                                       fc == 0, fc == NC - 1,
                                       tag=f"y{gi}_{cc}_{fc}")
                            tt(x_fm[cc][:, i * 1024:(i + 1) * 1024],
                               x_fm[cc][:, i * 1024:(i + 1) * 1024],
                               ps_y[:, :], ALU.add,
                               tag=f"yacc{gi}_{cc}_{i}")
                    if gi + 2 < 2 * E:
                        load_granule(gi + 2)

    def store_out(s):
        with nc.psum_tensor(psname("ps_st0"), [128, 1024], F32) as p0, \
             nc.psum_tensor(psname("ps_st1"), [128, 1024], F32) as p1:
            P.reg(p0); P.reg(p1)
            pss = [p0, p1]
            for t in range(NT):
                ps = pss[t % 2]
                stg = out_stage[t % 2]
                for c in range(NC):
                    tr(ps[:, c * 128:(c + 1) * 128],
                       x_fm[c][:, t * 128:(t + 1) * 128], ident_f32[:],
                       tag=f"otr{t}_{c}")
                copy(stg[:], ps[:, :], eng=("DVE" if t % 2 == 0 else "ACT"),
                     tag=f"ocp{t}")
                dma(f"g_os{t % 2}", y_out[s, t * 128:(t + 1) * 128, :], stg[:],
                    tag=f"ostore{t}")

    if "dbgz" in stages:
        dbg = nc.dram_tensor("dbg", [NC, 128, S], BF16, kind="ExternalOutput").ap()
    if "dbgw" in stages:
        dbgw = nc.dram_tensor("dbgw", [4, 128, 1024], BF16, kind="ExternalOutput").ap()
    if "dbgh" in stages:
        dbgh = nc.dram_tensor("dbgh", [NC, 128, S], BF16, kind="ExternalOutput").ap()
        dbghc = nc.dram_tensor("dbghc", [128, S], BF16, kind="ExternalOutput").ap()
        dbghw = nc.dram_tensor("dbghw", [2, 128, 1024], BF16, kind="ExternalOutput").ap()
    if "dbgc" in stages:
        dbgc = nc.dram_tensor("dbgc", [128, NT * E], F32, kind="ExternalOutput").ap()
        dbgs = nc.dram_tensor("dbgs", [128, 2 * NT], F32, kind="ExternalOutput").ap()
        dbgr = nc.dram_tensor("dbgr", [128, 16], F32, kind="ExternalOutput").ap()

    for s in range(nseq):
        load_and_ln1(s)
        if "attn" in stages:
            qkv_attn(s)
            if "dbgz" in stages and s == 0:
                for c in range(NC):
                    dma("g_os0", dbg[c], zh[c][:], tag=f"dbgz{c}")
        if "proj" in stages:
            outproj(s)
        if "router" in stages:
            router(s)
            if "dbgc" in stages and s == 0:
                dma("g_os0", dbgc, comb_tok[:], tag="dbgc")
                dma("g_os0", dbgs[:, 0:NT], rstd3_tok[:], tag="dbgs1")
                dma("g_os0", dbgs[:, NT:2 * NT], ct_tok[:], tag="dbgs2")
                dma("g_os0", dbgr[:, 0:8], bp_rep[:], tag="dbgr1")
                dma("g_os0", dbgr[:, 8:16], sp_rep[:], tag="dbgr2")
        if "ffn" in stages:
            ffn(s)
        store_out(s)

    P.schedule()
    if verbose:
        print("ops:", len(P.ops), "milestones:", P.ms_total,
              "dma groups:", P.dma_counts)

    eng_sems = {e: nc.alloc_semaphore(f"sem_{e}") for e in ENGS}
    grp_sems = {g: nc.alloc_semaphore(f"sem_{g}") for g in P.dma_counts}

    by_eng = {e: [] for e in ENGS}
    for o in P.ops:
        by_eng[o["eng"]].append(o)

    def emit_stream(handle, name):
        for o in by_eng[name]:
            for (src, val) in o["waits"]:
                sem = grp_sems[src[1]] if isinstance(src, tuple) else eng_sems[src]
                handle.wait_ge(sem, val)
            inst = o["fn"](handle)
            if o["dma_group"] is not None:
                inst.then_inc(grp_sems[o["dma_group"]], 16)
            elif o["milestone"] is not None:
                inst.then_inc(eng_sems[o["eng"]], 1)

    with nc.Block() as block:
        @block.tensor
        def _(t):
            emit_stream(t, "PE")

        @block.scalar
        def _(sc):
            emit_stream(sc, "ACT")

        @block.vector
        def _(v):
            emit_stream(v, "DVE")

        @block.gpsimd
        def _(gp):
            emit_stream(gp, "POOL")

        @block.sync
        def _(sy):
            emit_stream(sy, "SP")
            for g in ("g_os0", "g_os1"):
                sy.wait_ge(grp_sems[g], 16 * P.dma_counts[g])

    return nc


_NC_CACHE = None


def kernel(L0_attn_x, L0_attn_W_Q, L0_attn_W_K, L0_attn_W_V, L0_attn_W_O,
           L0_ffn_ln_gamma, L0_ffn_ln_beta, L0_ffn_gate_w, L0_ffn_w1, L0_ffn_w2):
    global _NC_CACHE
    if _NC_CACHE is None:
        _NC_CACHE = build_program()
    nc = _NC_CACHE
    x = np.ascontiguousarray(np.asarray(L0_attn_x, dtype=np.float32))
    common = {
        "wq": np.ascontiguousarray(np.asarray(L0_attn_W_Q, dtype=np.float32)),
        "wk": np.ascontiguousarray(np.asarray(L0_attn_W_K, dtype=np.float32)),
        "wv": np.ascontiguousarray(np.asarray(L0_attn_W_V, dtype=np.float32)),
        "wo": np.ascontiguousarray(np.asarray(L0_attn_W_O, dtype=np.float32)),
        "gamma": np.ascontiguousarray(
            np.asarray(L0_ffn_ln_gamma, dtype=np.float32)).reshape(1, D),
        "beta": np.ascontiguousarray(
            np.asarray(L0_ffn_ln_beta, dtype=np.float32)).reshape(1, D),
        "gw": np.ascontiguousarray(np.asarray(L0_ffn_gate_w, dtype=np.float32)),
        "w1": np.ascontiguousarray(np.asarray(L0_ffn_w1, dtype=np.float32)),
        "w2": np.ascontiguousarray(np.asarray(L0_ffn_w2, dtype=np.float32)),
    }
    in_maps = []
    for c in range(NCORE):
        m = dict(common)
        m["x"] = np.ascontiguousarray(x[c * RSEQ:(c + 1) * RSEQ])
        in_maps.append(m)
    res = run_bass_kernel_spmd(nc, in_maps, core_ids=list(range(NCORE)))
    return np.concatenate([r["out"] for r in res.results], axis=0)


# version 15
# speedup vs baseline: 1.0264x; 1.0264x over previous
import numpy as np
import ml_dtypes
import concourse.bass as bass
import concourse.mybir as mybir
from concourse.bass_utils import run_bass_kernel_spmd

F32 = mybir.dt.float32
BF16 = mybir.dt.bfloat16
AF = mybir.ActivationFunctionType
ALU = mybir.AluOpType
AX = mybir.AxisListType
BFNP = ml_dtypes.bfloat16

B, S, D, H, DH, E, FF = 32, 2048, 1024, 16, 64, 8, 2048
NCORE = 8
RSEQ = B // NCORE
NT = S // 128
NC = D // 128
EPS = 1e-5
INVD = 1.0 / D
SCALE = 1.0 / 8.0

ENGS = ("PE", "ACT", "DVE", "POOL", "SP")


class Prog:

    def __init__(self, nc):
        self.nc = nc
        self.ops = []
        self.ranges = {}
        self.partners = {}
        self.wstate = {}
        self.rstate = {}
        self.dma_counts = {}

    def reg(self, h):
        key = id(h)
        if key in self.ranges:
            return h
        try:
            ml = self.nc.lookup_mloc(h)
            mtype = ml.type
        except Exception:
            ml, mtype = None, None
        if mtype == "SB":
            space, lo = "SB", ml.addr
            hi = lo + _ppb(h)
        elif mtype == "PSUM":
            space = "PSUM"
            lo = ml.bank * 2048 + ml.addr
            hi = lo + _ppb(h)
        else:
            space, lo, hi = "DRAM:" + h.name, 0, 1
        self.ranges[key] = (space, lo, hi, h)
        self.partners[key] = set()
        self.wstate[key] = None
        self.rstate[key] = []
        for k2, (sp2, lo2, hi2, _h2) in self.ranges.items():
            if k2 != key and sp2 == space and lo < hi2 and lo2 < hi:
                self.partners[key].add(k2)
                self.partners[k2].add(key)
        return h

    def _keys(self, objs):
        out = []
        for o in objs:
            h = o.tensor if hasattr(o, "tensor") else o
            k = id(h)
            if k not in self.ranges:
                self.reg(h)
            out.append(k)
        return out

    def op(self, eng, fn, reads=(), writes=(), dma_group=None, tag=""):
        opid = len(self.ops)
        rk = self._keys(reads)
        wk = self._keys(writes)
        deps = set()
        for k in rk:
            is_psum = self.ranges[k][0] == "PSUM"
            for g in ({k} | self.partners[k]):
                if self.wstate[g] is not None:
                    deps.add(self.wstate[g])
                if is_psum:
                    by_eng = {}
                    for r in self.rstate[g]:
                        e_r = self.ops[r]["eng"]
                        if e_r != eng and by_eng.get(e_r, -1) < r:
                            by_eng[e_r] = r
                    deps.update(by_eng.values())
        for k in wk:
            for g in ({k} | self.partners[k]):
                if self.wstate[g] is not None:
                    deps.add(self.wstate[g])
                by_eng = {}
                for r in self.rstate[g]:
                    e_r = self.ops[r]["eng"]
                    if by_eng.get(e_r, -1) < r:
                        by_eng[e_r] = r
                deps.update(by_eng.values())
        deps.discard(opid)
        for k in wk:
            for g in ({k} | self.partners[k]):
                self.wstate[g] = opid
                self.rstate[g] = []
        for k in rk:
            self.rstate[k].append(opid)
        gcount = None
        if dma_group is not None:
            self.dma_counts[dma_group] = self.dma_counts.get(dma_group, 0) + 1
            gcount = self.dma_counts[dma_group]
        self.ops.append(dict(eng=eng, fn=fn, deps=deps, dma_group=dma_group,
                             gcount=gcount, tag=tag, milestone=None))
        return opid

    def schedule(self):
        ops = self.ops
        n = len(ops)
        needed = [False] * n
        for o in ops:
            for d in o["deps"]:
                p = ops[d]
                if p["dma_group"] is not None:
                    continue
                if p["eng"] == o["eng"] == "PE":
                    continue
                needed[d] = True
        ms = {e: 0 for e in ENGS}
        for i, o in enumerate(ops):
            if o["dma_group"] is None and needed[i]:
                ms[o["eng"]] += 1
                o["milestone"] = ms[o["eng"]]
        self.ms_total = ms
        know = {e: {} for e in ENGS}
        self_done = {e: 0 for e in ENGS}
        snaps_eng = {}
        snaps_dma = {}
        run_g = {}
        for o in ops:
            e = o["eng"]
            k = know[e]
            wmax = {}
            for d in sorted(o["deps"]):
                p = ops[d]
                if p["dma_group"] is not None:
                    src = ("g", p["dma_group"])
                    val = 16 * run_g.get(p["dma_group"], 0)
                    if k.get(src, 0) >= val:
                        continue
                    wmax[src] = max(wmax.get(src, 0), val)
                    for s2, v2 in snaps_dma.get((p["dma_group"], p["gcount"]),
                                                {}).items():
                        if k.get(s2, 0) < v2:
                            k[s2] = v2
                    if val > 0:
                        k[src] = max(k.get(src, 0), val)
                else:
                    if p["eng"] == e == "PE":
                        continue
                    m = p["milestone"]
                    if m is None:
                        continue
                    src = p["eng"]
                    if src == e:
                        if m <= self_done[e]:
                            continue
                        wmax[src] = max(wmax.get(src, 0), m)
                        self_done[e] = m
                    else:
                        if k.get(src, 0) >= m:
                            continue
                        wmax[src] = max(wmax.get(src, 0), m)
                        k[src] = max(k.get(src, 0), m)
                    for s2, v2 in snaps_eng.get((p["eng"], m), {}).items():
                        if k.get(s2, 0) < v2:
                            k[s2] = v2
            o["waits"] = sorted(wmax.items(), key=lambda x: str(x[0]))
            if o["dma_group"] is not None:
                run_g[o["dma_group"]] = run_g.get(o["dma_group"], 0) + 1
                snaps_dma[(o["dma_group"], o["gcount"])] = dict(k)
            elif o["milestone"] is not None:
                k2 = dict(k)
                k2[e] = o["milestone"]
                snaps_eng[(e, o["milestone"])] = k2
                know[e] = k2
        for e in ENGS:
            assert self.ms_total[e] < 60000, (e, self.ms_total[e])
        for g, c in self.dma_counts.items():
            assert 16 * c < 60000, (g, c)


def _ppb(h):
    free = 1
    for d in list(h.shape)[1:]:
        free *= d
    sz = 2 if h.dtype in (BF16, mybir.dt.float16) else 4
    return free * sz


_PSCNT = [0]


def build_program(verbose=False, nseq=RSEQ,
                  stages=("cast", "wot", "gate", "ln", "attn", "proj",
                          "router", "ffn")):
    nc = bass.Bass("TRN2", target_bir_lowering=False, debug=False,
                   dynamic_dma_scratch_size=2048)
    P = Prog(nc)

    def psname(base):
        _PSCNT[0] += 1
        return f"{base}_{_PSCNT[0]}"

    x_in = nc.dram_tensor("x", [RSEQ, S, D], F32, kind="ExternalInput").ap()
    y_out = nc.dram_tensor("out", [RSEQ, S, D], F32, kind="ExternalOutput").ap()
    wq = nc.dram_tensor("wq", [H, DH, D], F32, kind="ExternalInput").ap()
    wk = nc.dram_tensor("wk", [H, DH, D], F32, kind="ExternalInput").ap()
    wv = nc.dram_tensor("wv", [H, DH, D], F32, kind="ExternalInput").ap()
    wo = nc.dram_tensor("wo", [H, D, DH], F32, kind="ExternalInput").ap()
    gamma = nc.dram_tensor("gamma", [1, D], F32, kind="ExternalInput").ap()
    beta = nc.dram_tensor("beta", [1, D], F32, kind="ExternalInput").ap()
    gw = nc.dram_tensor("gw", [E, D], F32, kind="ExternalInput").ap()
    w1 = nc.dram_tensor("w1", [E, FF, D], F32, kind="ExternalInput").ap()
    w2 = nc.dram_tensor("w2", [E, D, FF], F32, kind="ExternalInput").ap()

    wq_bf = nc.dram_tensor("wq_bf", [H * DH, D], BF16, kind="Internal").ap()
    wk_bf = nc.dram_tensor("wk_bf", [H * DH, D], BF16, kind="Internal").ap()
    wv_bf = nc.dram_tensor("wv_bf", [H * DH, D], BF16, kind="Internal").ap()
    wot_bf = nc.dram_tensor("wot_bf", [H * DH, D], BF16, kind="Internal").ap()
    wo_bf = nc.dram_tensor("wo_bf", [H, D, DH], BF16, kind="Internal").ap()
    w1_bf = [nc.dram_tensor(f"w1_bf{e}", [FF, D], BF16, kind="Internal").ap()
             for e in range(E)]
    w2_bf = [nc.dram_tensor(f"w2_bf{e}", [D, FF], BF16, kind="Internal").ap()
             for e in range(E)]

    ident_bf_h = nc.inline_tensor(np.eye(128, dtype=BFNP), name="ident_bf")
    ident_f32_h = nc.inline_tensor(np.eye(128, dtype=np.float32), name="ident_f32")
    maskU_h = nc.inline_tensor(np.triu(np.ones((128, 128), dtype=BFNP)),
                               name="maskU")
    ones_row_bf_h = nc.inline_tensor(np.ones((1, 128), dtype=BFNP),
                                     name="ones_row_bf")
    ones_row_f32_h = nc.inline_tensor(np.ones((1, 128), dtype=np.float32),
                                      name="ones_row_f32")
    ones_col_bf_h = nc.inline_tensor(np.ones((128, 1), dtype=BFNP),
                                     name="ones_col_bf")
    basis_np = np.zeros((E, E * 128), dtype=BFNP)
    for e in range(E):
        basis_np[e, e * 128:(e + 1) * 128] = 1
    basis_h = nc.inline_tensor(basis_np, name="basis")
    constv_h = nc.inline_tensor(
        np.concatenate([np.zeros((128, 1), np.float32),
                        np.full((128, 1), EPS, np.float32)], axis=1),
        name="constv")

    A = nc.alloc_sbuf_tensor
    x_fm = [A(f"x_fm{c}", [128, S], F32) for c in range(NC)]
    xn_fm = [A(f"xn_fm{c}", [128, S], BF16) for c in range(NC)]
    zh = [A(f"zh{c}", [128, S], BF16) for c in range(NC)]
    vtok = [A(f"vtok{k}", [128, H * (DH + 1)], BF16) for k in range(NT)]
    q_c = A("q_c", [128, S], BF16)
    k_c = A("k_c", [128, S], BF16)
    attnT = A("attnT", [128, 2048], BF16)
    w1tA = [A(f"w1tA{c}", [128, 1024], BF16) for c in range(NC)]
    w2tA = [A(f"w2tA{c}", [128, 1024], BF16) for c in range(NC)]
    comb_rep = A("comb_rep", [128, S], BF16)
    comb_tok = A("comb_tok", [128, NT * E], F32)
    rstd3_tok = A("rstd3_tok", [128, NT], F32)
    ct_tok = A("ct_tok", [128, NT], F32)
    out_stage = [A(f"out_stage{i}", [128, D], F32) for i in range(2)]
    rt_scr = A("rt_scr", [128, 32], F32)
    cbuf = A("cbuf", [128, 2], F32)
    ident_bf = A("identbf_sb", [128, 128], BF16)
    ident_f32 = A("identf32_sb", [128, 128], F32)
    maskU = A("maskU_sb", [128, 128], BF16)
    ones_row_bf = A("onesrow_bf_sb", [1, 128], BF16)
    ones_row_f32 = A("onesrow_f32_sb", [1, 128], F32)
    ones_col_bf = A("onescol_bf_sb", [128, 1], BF16)
    basis_sb = A("basis_sb", [E, E * 128], BF16)
    gwT = [A(f"gwT{c}", [128, E], BF16) for c in range(NC)]
    sp_rep = A("sp_rep", [128, E], F32)
    bp_rep = A("bp_rep", [128, E], F32)

    def addr(h):
        return nc.lookup_mloc(h).addr

    def at(name, shape, dtype, base_h, delta=0):
        return nc.alloc_sbuf_tensor_at(name, shape, dtype,
                                       offset=addr(base_h) + delta)

    for i in range(1, NT):
        assert addr(vtok[i]) == addr(vtok[0]) + i * 2080

    gw_f32 = at("gw_f32", [E, D], F32, zh[2], 0)
    gamma_row = at("gamma_row", [1, D], F32, zh[3], 0)
    beta_row = at("beta_row", [1, D], F32, zh[4], 0)
    ph0_stage = at("ph0_stage", [128, 256], BF16, zh[5], 0)
    ph0_stage2 = at("ph0_stage2", [128, 256], BF16, zh[5], 1024)
    small_rows = at("small_rows", [8, 128], F32, zh[5], 512)
    r_rows = [at(f"r_row{i}", [1, 512], F32, zh[6], i * 2048) for i in range(2)]
    tmp_strip = at("tmp_strip", [128, 512], F32, zh[7], 0)
    sq_strip = [at("sq_strip0", [128, 512], BF16, zh[7], 2048),
                at("sq_strip1", [128, 512], BF16, zh[7], 3072)]
    cast_strip = [at("cast_strip0", [128, 512], BF16, zh[6], 0),
                  at("cast_strip1", [128, 512], BF16, zh[6], 1024)]

    x_stage = [at("x_stage0", [128, D], F32, q_c, 0),
               at("x_stage1", [128, D], F32, k_c, 0)]
    rowA_bf = at("rowA_bf", [1, 1024], BF16, comb_rep, 0)
    rowB_bf = at("rowB_bf", [1, 1024], BF16, comb_rep, 2048)
    repA_sb = at("repA_sb", [64, 1024], BF16, comb_rep, 0)
    repB_sb = at("repB_sb", [64, 1024], BF16, comb_rep, 2048)
    comb_T = at("comb_T", [E, S], BF16, attnT, 0)
    wsl = [[at(f"wsl{p}_{j}", [128, 1024], BF16, w2tA[p * 3 + j], 0)
            for j in range(3)] for p in range(2)]
    wot_sb = [at(f"wot_sb{c}", [128, 1024], BF16, w1tA[c], 0) for c in range(NC)]
    w1tB = [at(f"w1tB{c}", [128, 1024], BF16, vtok[0], c * 2048)
            for c in range(NC)]
    w2tB = [at(f"w2tB{c}", [128, 1024], BF16, vtok[8], c * 2048)
            for c in range(NC)]

    if verbose:
        print("SBUF bytes remaining:", nc.sbuf_bytes_remaining)
    assert nc.sbuf_bytes_remaining >= 0
    nc.const_aps.register(F32, 0.0, cbuf.ap()[:, 0:1])
    nc.const_aps.register(F32, EPS, cbuf.ap()[:, 1:2])

    def dma(group, out, in_, transpose=False, queue="SP", tag=""):
        def fn(eng, out=out, in_=in_, transpose=transpose):
            return eng.dma_start(out=out, in_=in_, transpose=transpose)
        P.op(queue, fn, reads=[in_], writes=[out], dma_group=group, tag=tag)

    def mm(out, lhsT, rhs, start, stop, tp=None, tag=""):
        def fn(eng, out=out, lhsT=lhsT, rhs=rhs, start=start, stop=stop, tp=tp):
            return eng.matmul(out=out, lhsT=lhsT, rhs=rhs, start=start,
                              stop=stop, tile_position=tp)
        P.op("PE", fn, reads=[lhsT, rhs], writes=[out], tag=tag)

    def tr(out, in_, ident, tag=""):
        def fn(eng, out=out, in_=in_, ident=ident):
            return eng.transpose(out=out, in_=in_, identity=ident)
        P.op("PE", fn, reads=[in_, ident], writes=[out], tag=tag)

    def act(out, in_, func, bias=0.0, scale=1.0, tag=""):
        reads = [in_] + ([bias] if hasattr(bias, "tensor") else [cbuf])
        def fn(e, out=out, in_=in_, func=func, bias=bias, scale=scale):
            return e.activation(out=out, in_=in_, func=func, bias=bias,
                                scale=scale)
        P.op("ACT", fn, reads=reads, writes=[out], tag=tag)

    def copy(out, in_, eng="DVE", tag=""):
        if eng == "ACT":
            def fn(e, out=out, in_=in_):
                return e.copy(out=out, in_=in_)
        else:
            def fn(e, out=out, in_=in_):
                return e.tensor_copy(out=out, in_=in_)
        P.op(eng, fn, reads=[in_], writes=[out], tag=tag)

    def tt(out, in0, in1, opk, eng="DVE", tag=""):
        def fn(e, out=out, in0=in0, in1=in1, opk=opk):
            return e.tensor_tensor(out=out, in0=in0, in1=in1, op=opk)
        P.op(eng, fn, reads=[in0, in1], writes=[out], tag=tag)

    def ts(out, in0, s1, s2, op0, op1=ALU.bypass, eng="DVE", tag=""):
        reads = [in0] + [s for s in (s1, s2) if hasattr(s, "tensor")]
        def fn(e, out=out, in0=in0, s1=s1, s2=s2, op0=op0, op1=op1):
            return e.tensor_scalar(out=out, in0=in0, scalar1=s1, scalar2=s2,
                                   op0=op0, op1=op1)
        P.op(eng, fn, reads=reads, writes=[out], tag=tag)

    def red(out, in_, opk, negate=False, tag=""):
        def fn(e, out=out, in_=in_, opk=opk, negate=negate):
            return e.tensor_reduce(out=out, in_=in_, axis=AX.X, op=opk,
                                   negate=negate)
        P.op("DVE", fn, reads=[in_], writes=[out], tag=tag)

    def recip(out, in_, tag=""):
        if out.dtype == mybir.dt.float32r:
            def fn(e, out=out, in_=in_):
                with nc.allow_low_precision(reason="f32r recip for replication"):
                    return e.reciprocal(out=out, in_=in_)
        else:
            def fn(e, out=out, in_=in_):
                return e.reciprocal(out=out, in_=in_)
        P.op("DVE", fn, reads=[in_], writes=[out], tag=tag)

    def memset(buf, val, tag=""):
        def fn(e, buf=buf, val=val):
            return e.memset(buf, val)
        P.op("DVE", fn, reads=[], writes=[buf], tag=tag)

    dma("g_const", ident_bf[:], ident_bf_h.ap())
    dma("g_const", ident_f32[:], ident_f32_h.ap())
    dma("g_const", maskU[:], maskU_h.ap())
    dma("g_const", ones_row_bf[:], ones_row_bf_h.ap())
    dma("g_const", ones_row_f32[:], ones_row_f32_h.ap())
    dma("g_const", ones_col_bf[:], ones_col_bf_h.ap())
    dma("g_const", basis_sb[:], basis_h.ap())
    dma("g_const", gw_f32[:], gw)
    dma("g_const", cbuf[:], constv_h.ap())
    dma("g_const", gamma_row[:], gamma)
    dma("g_const", beta_row[:], beta)

    if "cast" in stages:
        dma("g_ph0", wq_bf, wq.rearrange("h d c -> (h d) c"), queue="POOL")
        dma("g_ph0", wk_bf, wk.rearrange("h d c -> (h d) c"), queue="POOL")
        dma("g_ph0", wv_bf, wv.rearrange("h d c -> (h d) c"), queue="POOL")
        dma("g_ph0", wo_bf, wo, queue="POOL")
        for e in range(E):
            dma("g_ph0", w1_bf[e], w1[e], queue="POOL")
            dma("g_ph0", w2_bf[e], w2[e], queue="POOL")

    if "wot" in stages:
      with nc.psum_tensor(psname("ps_wot"), [64, 512], BF16) as ps_wot:
          P.reg(ps_wot)
          for a in range(H):
              for et in range(NC):
                  dma("g_ph0b", ph0_stage[:, 0:64],
                      wo_bf[a, et * 128:(et + 1) * 128, :])
                  sl = ps_wot[0:64, (et % 4) * 128:(et % 4) * 128 + 128]
                  tr(sl, ph0_stage[:, 0:64], ident_bf[:], tag=f"wotT{a}_{et}")
                  copy(ph0_stage[0:64, 128:256], sl, eng="ACT")
                  dma("g_ph0b",
                      wot_bf[a * 64:(a + 1) * 64, et * 128:(et + 1) * 128],
                      ph0_stage[0:64, 128:256])

    if "gate" in stages:
      with nc.psum_tensor(psname("ps_g"), [128, 1024], F32) as ps_g:
          P.reg(ps_g)
          for nch in range(2):
              mm(ps_g[0:8, nch * 512:(nch + 1) * 512], ones_row_f32[0:1, 0:8],
                 gamma_row[0:1, nch * 512:(nch + 1) * 512], True, True, tag="grep")
          tt(gw_f32[:], gw_f32[:], ps_g[0:8, 0:1024], ALU.mult, tag="gwp")
      gw_bf = at("gw_bf", [E, D], BF16, zh[0], 0)
      P.reg(gw_bf)
      copy(gw_bf[:], gw_f32[:], eng="ACT", tag="gwbf")
      with nc.psum_tensor(psname("ps_gt"), [128, 512], BF16) as ps_gt:
          P.reg(ps_gt)
          for c in range(NC):
              tr(ps_gt[:, c * 8:(c + 1) * 8], gw_bf[0:8, c * 128:(c + 1) * 128],
                 ident_bf[0:8, 0:8], tag=f"gwT{c}")
          for c in range(NC):
              copy(gwT[c][:], ps_gt[:, c * 8:(c + 1) * 8], eng="DVE")
      beta_bf = at("beta_bf", [1, D], BF16, zh[1], 0)
      beta_fm = at("beta_fm", [128, 8], BF16, zh[1], 2048)
      P.reg(beta_bf); P.reg(beta_fm)
      copy(beta_bf[:], beta_row[:], eng="ACT", tag="betabf")
      with nc.psum_tensor(psname("ps_bt"), [128, 512], BF16) as ps_bt:
          P.reg(ps_bt)
          for c in range(NC):
              tr(ps_bt[:, 2 * c:2 * c + 1], beta_bf[0:1, c * 128:(c + 1) * 128],
                 ident_bf[0:1, 0:1], tag=f"betaT{c}")
          copy(beta_fm[:],
               ps_bt.ap()[:, 0:16].rearrange("p (a b) -> p a b", b=2)[:, :, 0],
               eng="DVE")
      with nc.psum_tensor(psname("ps_bs"), [8, 512], F32) as ps_bs:
          P.reg(ps_bs)
          for c in range(NC):
              mm(ps_bs[0:8, 0:1], gwT[c][:], beta_fm[:, c:c + 1],
                 c == 0, c == NC - 1, tag="bp")
          for c in range(NC):
              mm(ps_bs[0:8, 1:2], gwT[c][:], ones_col_bf[:],
                 c == 0, c == NC - 1, tag="sp")
          copy(small_rows[0:8, 0:2], ps_bs[0:8, 0:2], eng="DVE")
      with nc.psum_tensor(psname("ps_br"), [128, 512], F32) as ps_br:
          P.reg(ps_br)
          tr(ps_br[0:1, 0:8], small_rows[0:8, 0:1], ident_f32[0:8, 0:8], tag="bT")
          tr(ps_br[0:1, 8:16], small_rows[0:8, 1:2], ident_f32[0:8, 0:8], tag="sT")
          copy(small_rows[0:1, 8:24], ps_br[0:1, 0:16], eng="DVE")
          mm(ps_br[:, 16:24], ones_row_f32[0:1, :], small_rows[0:1, 8:16],
             True, True, tag="bprep")
          mm(ps_br[:, 24:32], ones_row_f32[0:1, :], small_rows[0:1, 16:24],
             True, True, tag="sprep")
          copy(bp_rep[:], ps_br[:, 16:24], eng="DVE")
          copy(sp_rep[:], ps_br[:, 24:32], eng="DVE")


    def ln_to(src, dst, src_f32, stats_out=False):
        with nc.psum_tensor(psname("ps_sum"), [1, 512], F32) as ps_sum, \
             nc.psum_tensor(psname("ps_sq"), [1, 512], F32) as ps_sq, \
             nc.psum_tensor(psname("ps_mu"), [128, 512], F32) as ps_mu, \
             nc.psum_tensor(psname("ps_rs"), [128, 512], F32) as ps_rs:
            P.reg(ps_sum); P.reg(ps_sq); P.reg(ps_mu); P.reg(ps_rs)
            for st in range(4):
                sl = slice(st * 512, (st + 1) * 512)
                for c in range(NC):
                    if src_f32:
                        xbf = cast_strip[c % 2][:]
                        act(xbf, src[c][:, sl], AF.Copy, tag=f"lncast{st}_{c}")
                    else:
                        xbf = src[c][:, sl]
                    mm(ps_sum[0:1, :], ones_col_bf[:], xbf, c == 0, c == NC - 1,
                       tag=f"lnsum{st}_{c}")
                    tt(sq_strip[c % 2][:], xbf, xbf, ALU.mult,
                       tag=f"lnsq{st}_{c}")
                    mm(ps_sq[0:1, :], ones_col_bf[:], sq_strip[c % 2][:],
                       c == 0, c == NC - 1, tag=f"lnsq2{st}_{c}")
                ts(r_rows[0][:], ps_sum[0:1, :], INVD, None, ALU.mult,
                   tag=f"lnmu{st}")
                ts(r_rows[1][:], ps_sq[0:1, :], INVD, None, ALU.mult,
                   tag=f"lnmsq{st}")
                tt(tmp_strip[0:1, :], r_rows[0][:], r_rows[0][:], ALU.mult,
                   tag=f"lnmu2{st}")
                tt(r_rows[1][:], r_rows[1][:], tmp_strip[0:1, :], ALU.subtract,
                   tag=f"lnvar{st}")
                act(tmp_strip[0:1, :], r_rows[1][:], AF.Ln, bias=EPS,
                    tag=f"lnln{st}")
                act(r_rows[1][:], tmp_strip[0:1, :], AF.Exp, scale=-0.5,
                    tag=f"lnexp{st}")
                if stats_out:
                    tt(tmp_strip[0:1, :], r_rows[0][:], r_rows[1][:], ALU.mult,
                       tag=f"ct{st}")
                    with nc.psum_tensor(psname("ps_stt"), [128, 8], F32) as pst:
                        P.reg(pst)
                        for q in range(4):
                            tr(pst[:, q * 2:q * 2 + 1],
                               r_rows[1][0:1, q * 128:(q + 1) * 128],
                               ident_f32[0:1, 0:1], tag=f"rsT{st}_{q}")
                            tr(pst[:, q * 2 + 1:q * 2 + 2],
                               tmp_strip[0:1, q * 128:(q + 1) * 128],
                               ident_f32[0:1, 0:1], tag=f"ctT{st}_{q}")
                        for q in range(4):
                            tile = st * 4 + q
                            copy(rstd3_tok[:, tile:tile + 1],
                                 pst[:, q * 2:q * 2 + 1])
                            copy(ct_tok[:, tile:tile + 1],
                                 pst[:, q * 2 + 1:q * 2 + 2])
                if dst is not None:
                    mm(ps_mu[:, :], ones_row_f32[0:1, :], r_rows[0][:],
                       True, True, tag=f"murep{st}")
                    mm(ps_rs[:, :], ones_row_f32[0:1, :], r_rows[1][:],
                       True, True, tag=f"rsrep{st}")
                    for c in range(NC):
                        tt(tmp_strip[:], src[c][:, sl], ps_mu[:, :],
                           ALU.subtract, tag=f"lnsub{st}_{c}")
                        tt(dst[c][:, sl], tmp_strip[:], ps_rs[:, :], ALU.mult,
                           tag=f"lnmul{st}_{c}")

    def load_and_ln1(s):
        with nc.psum_tensor(psname("ps_ld0"), [128, 1024], F32) as p0, \
             nc.psum_tensor(psname("ps_ld1"), [128, 1024], F32) as p1:
            P.reg(p0); P.reg(p1)
            pss = [p0, p1]
            for t in range(NT):
                stg = x_stage[t % 2]
                dma(f"g_xs{t % 2}", stg[:], x_in[s, t * 128:(t + 1) * 128, :],
                    tag=f"xload{t}")
                ps = pss[t % 2]
                for c in range(NC):
                    tr(ps[:, c * 128:(c + 1) * 128], stg[:, c * 128:(c + 1) * 128],
                       ident_f32[:], tag=f"xtr{t}_{c}")
                for c in range(NC):
                    copy(x_fm[c][:, t * 128:(t + 1) * 128],
                         ps[:, c * 128:(c + 1) * 128],
                         eng=("DVE" if c % 2 == 0 else "ACT"), tag=f"xcp{t}_{c}")
        if "ln" in stages:
            ln_to(x_fm, xn_fm, src_f32=True)

    def qkv_attn(s):
        for c in range(NC):
            dma("g_wot", wot_sb[c][:], wot_bf[c * 128:(c + 1) * 128, :],
                tag=f"wot{c}")
        for kt in range(NT):
            memset(vtok[kt].ap().rearrange("p (h v) -> p h v", v=DH + 1)
                   [:, :, DH], 1.0, tag=f"vones{kt}")
        def wsl_load(g):
            par = g % 2
            for c in range(NC):
                dma(f"g_wsl{par}", wsl[par][0][:, c * 128:(c + 1) * 128],
                    wq_bf[g * 128:(g + 1) * 128, c * 128:(c + 1) * 128],
                    transpose=True, tag=f"wq{g}_{c}")
                dma(f"g_wsl{par}", wsl[par][1][:, c * 128:(c + 1) * 128],
                    wk_bf[g * 128:(g + 1) * 128, c * 128:(c + 1) * 128],
                    transpose=True, tag=f"wk{g}_{c}")
                dma(f"g_wsl{par}", wsl[par][2][:, c * 128:(c + 1) * 128],
                    wv_bf[g * 128:(g + 1) * 128, c * 128:(c + 1) * 128],
                    transpose=True, tag=f"wv{g}_{c}")

        wsl_load(0)
        for g in range(NC):
            par = g % 2
            with nc.psum_tensor(psname("ps_q"), [128, S], F32) as ps_q, \
                 nc.psum_tensor(psname("ps_k"), [128, S], F32) as ps_k:
                P.reg(ps_q); P.reg(ps_k)
                for c in range(NC):
                    for nch in range(4):
                        mm(ps_q[:, nch * 512:(nch + 1) * 512],
                           wsl[par][0][:, c * 128:(c + 1) * 128],
                           xn_fm[c][:, nch * 512:(nch + 1) * 512],
                           c == 0, c == NC - 1, tag=f"qmm{g}")
                for i in range(2):
                    copy(q_c[:, i * 1024:(i + 1) * 1024],
                         ps_q[:, i * 1024:(i + 1) * 1024], eng="ACT",
                         tag=f"qev{g}")
                for c in range(NC):
                    for nch in range(4):
                        mm(ps_k[:, nch * 512:(nch + 1) * 512],
                           wsl[par][1][:, c * 128:(c + 1) * 128],
                           xn_fm[c][:, nch * 512:(nch + 1) * 512],
                           c == 0, c == NC - 1, tag=f"kmm{g}")
                for i in range(2):
                    copy(k_c[:, i * 1024:(i + 1) * 1024],
                         ps_k[:, i * 1024:(i + 1) * 1024], eng="ACT",
                         tag=f"kev{g}")
            with nc.psum_tensor(psname("ps_v0"), [128, 128], F32) as pv0, \
                 nc.psum_tensor(psname("ps_v1"), [128, 128], F32) as pv1, \
                 nc.psum_tensor(psname("ps_v2"), [128, 128], F32) as pv2, \
                 nc.psum_tensor(psname("ps_v3"), [128, 128], F32) as pv3:
                pvs = [pv0, pv1, pv2, pv3]
                for p_ in pvs:
                    P.reg(p_)
                for kt in range(NT):
                    sl_ps = pvs[kt % 4][:, :]
                    for c in range(NC):
                        mm(sl_ps, xn_fm[c][:, kt * 128:(kt + 1) * 128],
                           wsl[par][2][:, c * 128:(c + 1) * 128],
                           c == 0, c == NC - 1,
                           tag=f"vmm{g}_{kt}")
                    vv = vtok[kt].ap().rearrange("p (h v) -> p h v", v=DH + 1)
                    copy(vv[:, 2 * g:2 * g + 2, 0:DH],
                         sl_ps.rearrange("p (h v) -> p h v", v=DH)[:, 0:2, :],
                         eng="DVE", tag=f"vev{g}_{kt}")
            if g + 1 < NC:
                wsl_load(g + 1)
            attn_pair(s, g)

    def attn_pair(s, g):
        with nc.psum_tensor(psname("ps_s"), [128, 2048], F32) as ps_s, \
             nc.psum_tensor(psname("ps_za"), [65, 1024], F32) as ps_za, \
             nc.psum_tensor(psname("ps_zb"), [65, 1024], F32) as ps_zb:
            P.reg(ps_s); P.reg(ps_za); P.reg(ps_zb)
            psz = [ps_za, ps_zb]
            for half in range(2):
                h_lo = half * 1024
                nkb = 8 * (half + 1)
                for kb in range(nkb):
                    t1_lo = max(kb * 128 - h_lo, 0)
                    for hh in range(2):
                        base = hh * 64
                        col = t1_lo
                        while col < 1024:
                            w = min(512 - col % 512, 1024 - col)
                            mm(ps_s[:, hh * 1024 + col:hh * 1024 + col + w],
                               q_c[base:base + 64, kb * 128:(kb + 1) * 128],
                               k_c[base:base + 64, h_lo + col:h_lo + col + w],
                               True, True, tp=(base, 0),
                               tag=f"sc{g}_{half}_{kb}_{hh}")
                            col += w
                    wv_ = 1024 - t1_lo
                    exp_in = ps_s.ap().rearrange("p (h t) -> p h t", h=2)[
                        :, :, t1_lo:1024]
                    exp_out = attnT.ap().rearrange("p (h t) -> p h t", h=2)[
                        :, :, t1_lo:1024]
                    act(exp_out, exp_in, AF.Exp, scale=SCALE,
                        tag=f"exp{g}_{half}_{kb}")
                    if kb * 128 >= h_lo:
                        dcol = kb * 128 - h_lo
                        for hh in range(2):
                            tt(attnT[:, hh * 1024 + dcol:hh * 1024 + dcol + 128],
                               attnT[:, hh * 1024 + dcol:hh * 1024 + dcol + 128],
                               maskU[:], ALU.mult, tag=f"msk{g}_{half}_{kb}_{hh}")
                    for hh in range(2):
                        head = 2 * g + hh
                        ps_z = psz[hh]
                        vvh = vtok[kb][:, head * (DH + 1):(head + 1) * (DH + 1)]
                        col = t1_lo
                        while col < 1024:
                            w = min(512 - col % 512, 1024 - col)
                            mm(ps_z[:, col:col + w], vvh,
                               attnT[:, hh * 1024 + col:hh * 1024 + col + w],
                               kb == 0, kb == nkb - 1,
                               tag=f"av{g}_{half}_{kb}_{hh}")
                            col += w
                recip(out_stage[0][0:1, 0:1024], ps_za[64:65, :],
                      tag=f"recipA{g}_{half}")
                recip(out_stage[1][0:1, 0:1024], ps_zb[64:65, :],
                      tag=f"recipB{g}_{half}")
                copy(rowA_bf[:], out_stage[0][0:1, 0:1024], eng="ACT",
                     tag=f"rcA{g}_{half}")
                copy(rowB_bf[:], out_stage[1][0:1, 0:1024], eng="ACT",
                     tag=f"rcB{g}_{half}")
                for i in range(2):
                    mm(ps_s[0:64, i * 512:(i + 1) * 512],
                       ones_row_bf[0:1, 0:64], rowA_bf[0:1, i * 512:(i + 1) * 512],
                       True, True, tag=f"rrepA{g}_{half}_{i}")
                for i in range(2):
                    mm(ps_s[0:64, 1024 + i * 512:1024 + (i + 1) * 512],
                       ones_row_bf[0:1, 0:64], rowB_bf[0:1, i * 512:(i + 1) * 512],
                       True, True, tag=f"rrepB{g}_{half}_{i}")
                copy(repA_sb[:], ps_s[0:64, 0:1024], eng="ACT",
                     tag=f"rrcpA{g}_{half}")
                copy(repB_sb[:], ps_s[0:64, 1024:2048], eng="ACT",
                     tag=f"rrcpB{g}_{half}")
                tt(zh[g][0:64, half * 1024:(half + 1) * 1024],
                   ps_za[0:64, :], repA_sb[:], ALU.mult,
                   tag=f"znormA{g}_{half}")
                tt(zh[g][64:128, half * 1024:(half + 1) * 1024],
                   ps_zb[0:64, :], repB_sb[:], ALU.mult,
                   tag=f"znormB{g}_{half}")

    def outproj(s):
        with nc.psum_tensor(psname("ps_o0"), [128, S], F32) as p0, \
             nc.psum_tensor(psname("ps_o1"), [128, S], F32) as p1:
            P.reg(p0); P.reg(p1)
            pss = [p0, p1]
            for ec in range(NC):
                ps = pss[ec % 2]
                for ac in range(NC):
                    for nch in range(4):
                        mm(ps[:, nch * 512:(nch + 1) * 512],
                           wot_sb[ac][:, ec * 128:(ec + 1) * 128],
                           zh[ac][:, nch * 512:(nch + 1) * 512],
                           ac == 0, ac == NC - 1, tag=f"omm{ec}")
                for i in range(2):
                    tt(x_fm[ec][:, i * 1024:(i + 1) * 1024],
                       x_fm[ec][:, i * 1024:(i + 1) * 1024],
                       ps[:, i * 1024:(i + 1) * 1024], ALU.add,
                       tag=f"ores{ec}_{i}")

    def router(s):
        ln_to(x_fm, xn_fm, src_f32=True)
        ln_to(xn_fm, None, src_f32=False, stats_out=True)
        with nc.psum_tensor(psname("ps_lg"), [128, 128], F32) as ps_lg, \
             nc.psum_tensor(psname("ps_ct"), [8, 512], BF16) as ps_ct:
            P.reg(ps_lg); P.reg(ps_ct)
            lg_sb = at(psname("lg_sb"), [128, NT * E], F32, comb_rep, 0)
            P.reg(lg_sb)
            for t in range(NT):
                for c in range(NC):
                    mm(ps_lg[:, t * 8:(t + 1) * 8],
                       xn_fm[c][:, t * 128:(t + 1) * 128],
                       gwT[c][:], c == 0, c == NC - 1, tag=f"lg{t}")
                ts(lg_sb[:, t * 8:(t + 1) * 8], ps_lg[:, t * 8:(t + 1) * 8],
                   rstd3_tok[:, t:t + 1], None, ALU.mult, tag=f"lga{t}")
            for t in range(NT):
                cs = comb_tok[:, t * 8:(t + 1) * 8]
                lgs = lg_sb[:, t * 8:(t + 1) * 8]
                ts(rt_scr[:, 0:8], sp_rep[:], ct_tok[:, t:t + 1], None,
                   ALU.mult, tag=f"lgb{t}")
                tt(cs, lgs, rt_scr[:, 0:8], ALU.subtract, tag=f"lgc{t}")
                tt(cs, cs, bp_rep[:], ALU.add, tag=f"lgd{t}")
                nm = rt_scr[:, 8:9]
                red(nm, cs, ALU.max, negate=True, tag=f"smm{t}")
                act(cs, cs, AF.Exp, bias=nm, tag=f"sme{t}")
                sm = rt_scr[:, 9:10]
                red(sm, cs, ALU.add, tag=f"sms{t}")
                rc = rt_scr[:, 10:11]
                recip(rc, sm, tag=f"smr{t}")
                ts(cs, cs, rc, None, ALU.mult, tag=f"smn{t}")
                m1 = rt_scr[:, 11:12]
                red(m1, cs, ALU.max, tag=f"t2a{t}")
                eq = rt_scr[:, 16:24]
                ts(eq, cs, m1, None, ALU.is_ge, tag=f"t2b{t}")
                ts(eq, eq, -1.0, 1.0, ALU.mult, ALU.add, tag=f"t2c{t}")
                tt(eq, eq, cs, ALU.mult, tag=f"t2d{t}")
                m2 = rt_scr[:, 12:13]
                red(m2, eq, ALU.max, tag=f"t2e{t}")
                ts(eq, cs, m2, None, ALU.is_ge, tag=f"t2f{t}")
                tt(cs, cs, eq, ALU.mult, tag=f"t2g{t}")
                cbf = rt_scr.ap().bitcast(BF16)[:, 48:56]
                copy(cbf, cs, eng="DVE", tag=f"cbf{t}")
                tr(ps_ct[0:8, (t % 4) * 128:(t % 4) * 128 + 128], cbf,
                   ident_bf[:], tag=f"cT{t}")
                copy(comb_T[0:8, t * 128:(t + 1) * 128],
                     ps_ct[0:8, (t % 4) * 128:(t % 4) * 128 + 128], eng="DVE",
                     tag=f"cTc{t}")

    def ffn(s):
        def gbufs(gi):
            return (w1tB, w2tB) if gi % 2 == 0 else (w1tA, w2tA)

        def load_granule(gi):
            e, half = gi // 2, gi % 2
            w1b, w2b = gbufs(gi)
            grp = f"g_gr{gi % 2}"
            for c in range(NC):
                dma(grp, w1b[c][:],
                    w1_bf[e][half * 1024:(half + 1) * 1024,
                             c * 128:(c + 1) * 128], transpose=True,
                    tag=f"w1g{gi}_{c}")
            for fc in range(NC):
                dma(grp, w2b[fc][:],
                    w2_bf[e][:, half * 1024 + fc * 128:
                             half * 1024 + (fc + 1) * 128], transpose=True,
                    tag=f"w2g{gi}_{fc}")

        load_granule(0)
        load_granule(1)
        if "dbgw" in stages and s == 0:
            dma("g_os0", dbgw[0], w1tB[0][:], tag="dbgw0")
            dma("g_os0", dbgw[1], w1tB[5][:], tag="dbgw1")
            dma("g_os0", dbgw[2], w2tB[0][:], tag="dbgw2")
            dma("g_os0", dbgw[3], w1tA[0][:], tag="dbgw3")
        with nc.psum_tensor(psname("ps_h0"), [128, 1024], F32) as ph0, \
             nc.psum_tensor(psname("ps_h1"), [128, 1024], F32) as ph1, \
             nc.psum_tensor(psname("ps_y0"), [128, 1024], F32) as py0, \
             nc.psum_tensor(psname("ps_y1"), [128, 1024], F32) as py1:
            P.reg(ph0); P.reg(ph1); P.reg(py0); P.reg(py1)
            psh = [ph0, ph1]
            psy = [py0, py1]
            for e in range(E):
                for i in range(4):
                    mm(ph0[:, (i % 2) * 512:(i % 2) * 512 + 512],
                       basis_sb[0:8, e * 128:(e + 1) * 128],
                       comb_T[0:8, i * 512:(i + 1) * 512], True, True,
                       tag=f"crep{e}_{i}")
                    copy(comb_rep[:, i * 512:(i + 1) * 512],
                         ph0[:, (i % 2) * 512:(i % 2) * 512 + 512], eng="DVE",
                         tag=f"crepc{e}_{i}")
                for half in range(2):
                    gi = e * 2 + half
                    w1b, w2b = gbufs(gi)
                    if "dbgh" in stages and s == 0 and e == 0 and half == 0:
                        pass
                    for fc in range(NC):
                        for tb in range(2):
                            ps = psh[(fc * 2 + tb) % 2]
                            for th in range(2):
                                for c in range(NC):
                                    mm(ps[:, th * 512:(th + 1) * 512],
                                       w1b[c][:, fc * 128:(fc + 1) * 128],
                                       xn_fm[c][:, tb * 1024 + th * 512:
                                                tb * 1024 + (th + 1) * 512],
                                       c == 0, c == NC - 1,
                                       tag=f"h{gi}_{fc}_{tb}_{th}")
                            act(zh[fc][:, tb * 1024:(tb + 1) * 1024], ps[:, :],
                                AF.Gelu, tag=f"gelu{gi}_{fc}_{tb}")
                        tt(zh[fc][:], zh[fc][:], comb_rep[:], ALU.mult,
                           tag=f"hscale{gi}_{fc}")
                    if "dbgh" in stages and s == 0 and e == 0 and half == 0:
                        for c2 in range(NC):
                            dma("g_os0", dbgh[c2], zh[c2][:], tag=f"dbgh{c2}")
                        dma("g_os0", dbghc, comb_rep[:], tag="dbghc")
                        dma("g_os0", dbghw[0], w1tB[0][:], tag="dbghw0")
                        dma("g_os0", dbghw[1], xn_fm[0][:, 0:1024], tag="dbghw1")
                    for cc in range(NC):
                        for i in range(2):
                            ps_y = psy[(cc * 2 + i) % 2]
                            for fc in range(NC):
                                for nch in range(2):
                                    mm(ps_y[:, nch * 512:(nch + 1) * 512],
                                       w2b[fc][:, cc * 128:(cc + 1) * 128],
                                       zh[fc][:, i * 1024 + nch * 512:
                                               i * 1024 + (nch + 1) * 512],
                                       fc == 0, fc == NC - 1,
                                       tag=f"y{gi}_{cc}_{fc}")
                            tt(x_fm[cc][:, i * 1024:(i + 1) * 1024],
                               x_fm[cc][:, i * 1024:(i + 1) * 1024],
                               ps_y[:, :], ALU.add,
                               tag=f"yacc{gi}_{cc}_{i}")
                    if gi + 2 < 2 * E:
                        load_granule(gi + 2)

    def store_out(s):
        with nc.psum_tensor(psname("ps_st0"), [128, 1024], F32) as p0, \
             nc.psum_tensor(psname("ps_st1"), [128, 1024], F32) as p1:
            P.reg(p0); P.reg(p1)
            pss = [p0, p1]
            for t in range(NT):
                ps = pss[t % 2]
                stg = out_stage[t % 2]
                for c in range(NC):
                    tr(ps[:, c * 128:(c + 1) * 128],
                       x_fm[c][:, t * 128:(t + 1) * 128], ident_f32[:],
                       tag=f"otr{t}_{c}")
                copy(stg[:], ps[:, :], eng=("DVE" if t % 2 == 0 else "ACT"),
                     tag=f"ocp{t}")
                dma(f"g_os{t % 2}", y_out[s, t * 128:(t + 1) * 128, :], stg[:],
                    tag=f"ostore{t}")

    if "dbgz" in stages:
        dbg = nc.dram_tensor("dbg", [NC, 128, S], BF16, kind="ExternalOutput").ap()
    if "dbgw" in stages:
        dbgw = nc.dram_tensor("dbgw", [4, 128, 1024], BF16, kind="ExternalOutput").ap()
    if "dbgh" in stages:
        dbgh = nc.dram_tensor("dbgh", [NC, 128, S], BF16, kind="ExternalOutput").ap()
        dbghc = nc.dram_tensor("dbghc", [128, S], BF16, kind="ExternalOutput").ap()
        dbghw = nc.dram_tensor("dbghw", [2, 128, 1024], BF16, kind="ExternalOutput").ap()
    if "dbgc" in stages:
        dbgc = nc.dram_tensor("dbgc", [128, NT * E], F32, kind="ExternalOutput").ap()
        dbgs = nc.dram_tensor("dbgs", [128, 2 * NT], F32, kind="ExternalOutput").ap()
        dbgr = nc.dram_tensor("dbgr", [128, 16], F32, kind="ExternalOutput").ap()

    for s in range(nseq):
        load_and_ln1(s)
        if "attn" in stages:
            qkv_attn(s)
            if "dbgz" in stages and s == 0:
                for c in range(NC):
                    dma("g_os0", dbg[c], zh[c][:], tag=f"dbgz{c}")
        if "proj" in stages:
            outproj(s)
        if "router" in stages:
            router(s)
            if "dbgc" in stages and s == 0:
                dma("g_os0", dbgc, comb_tok[:], tag="dbgc")
                dma("g_os0", dbgs[:, 0:NT], rstd3_tok[:], tag="dbgs1")
                dma("g_os0", dbgs[:, NT:2 * NT], ct_tok[:], tag="dbgs2")
                dma("g_os0", dbgr[:, 0:8], bp_rep[:], tag="dbgr1")
                dma("g_os0", dbgr[:, 8:16], sp_rep[:], tag="dbgr2")
        if "ffn" in stages:
            ffn(s)
        store_out(s)

    P.schedule()
    if verbose:
        print("ops:", len(P.ops), "milestones:", P.ms_total,
              "dma groups:", P.dma_counts)

    eng_sems = {e: nc.alloc_semaphore(f"sem_{e}") for e in ENGS}
    grp_sems = {g: nc.alloc_semaphore(f"sem_{g}") for g in P.dma_counts}

    by_eng = {e: [] for e in ENGS}
    for o in P.ops:
        by_eng[o["eng"]].append(o)

    def emit_stream(handle, name):
        for o in by_eng[name]:
            for (src, val) in o["waits"]:
                sem = grp_sems[src[1]] if isinstance(src, tuple) else eng_sems[src]
                handle.wait_ge(sem, val)
            inst = o["fn"](handle)
            if o["dma_group"] is not None:
                inst.then_inc(grp_sems[o["dma_group"]], 16)
            elif o["milestone"] is not None:
                inst.then_inc(eng_sems[o["eng"]], 1)

    with nc.Block() as block:
        @block.tensor
        def _(t):
            emit_stream(t, "PE")

        @block.scalar
        def _(sc):
            emit_stream(sc, "ACT")

        @block.vector
        def _(v):
            emit_stream(v, "DVE")

        @block.gpsimd
        def _(gp):
            emit_stream(gp, "POOL")

        @block.sync
        def _(sy):
            emit_stream(sy, "SP")
            for g in ("g_os0", "g_os1"):
                sy.wait_ge(grp_sems[g], 16 * P.dma_counts[g])

    return nc


_NC_CACHE = None


def kernel(L0_attn_x, L0_attn_W_Q, L0_attn_W_K, L0_attn_W_V, L0_attn_W_O,
           L0_ffn_ln_gamma, L0_ffn_ln_beta, L0_ffn_gate_w, L0_ffn_w1, L0_ffn_w2):
    global _NC_CACHE
    if _NC_CACHE is None:
        _NC_CACHE = build_program()
    nc = _NC_CACHE
    x = np.ascontiguousarray(np.asarray(L0_attn_x, dtype=np.float32))
    common = {
        "wq": np.ascontiguousarray(np.asarray(L0_attn_W_Q, dtype=np.float32)),
        "wk": np.ascontiguousarray(np.asarray(L0_attn_W_K, dtype=np.float32)),
        "wv": np.ascontiguousarray(np.asarray(L0_attn_W_V, dtype=np.float32)),
        "wo": np.ascontiguousarray(np.asarray(L0_attn_W_O, dtype=np.float32)),
        "gamma": np.ascontiguousarray(
            np.asarray(L0_ffn_ln_gamma, dtype=np.float32)).reshape(1, D),
        "beta": np.ascontiguousarray(
            np.asarray(L0_ffn_ln_beta, dtype=np.float32)).reshape(1, D),
        "gw": np.ascontiguousarray(np.asarray(L0_ffn_gate_w, dtype=np.float32)),
        "w1": np.ascontiguousarray(np.asarray(L0_ffn_w1, dtype=np.float32)),
        "w2": np.ascontiguousarray(np.asarray(L0_ffn_w2, dtype=np.float32)),
    }
    in_maps = []
    for c in range(NCORE):
        m = dict(common)
        m["x"] = np.ascontiguousarray(x[c * RSEQ:(c + 1) * RSEQ])
        in_maps.append(m)
    res = run_bass_kernel_spmd(nc, in_maps, core_ids=list(range(NCORE)))
    return np.concatenate([r["out"] for r in res.results], axis=0)
